# version 18
# baseline (speedup 1.0000x reference)
import contextlib

import numpy as np
import concourse.bass as bass
import concourse.mybir as mybir
from concourse.bass_utils import run_bass_kernel_spmd

F32 = mybir.dt.float32
BF16 = mybir.dt.bfloat16
I32 = mybir.dt.int32
AF = mybir.ActivationFunctionType
ALU = mybir.AluOpType
AX = mybir.AxisListType

D = 1024
DK = 256
DV = 512
TILE = 512
NTC = 4
MAGIC = 12582912.0
LN16 = 2.772588722239781
LN127 = 4.844187086458591
EPS = 1e-5
FLAG = 77777

SEMS = {}
_DEBUG = False
_CHN = {"V": 0, "S": 0}


def _chain(eng_key, eng, instr):
    _CHN[eng_key] += 1
    instr.then_inc(SEMS["X" + eng_key], 1)
    eng.wait_ge(SEMS["X" + eng_key], _CHN[eng_key])


class Plan:
    def __init__(self):
        self.events = []
        self.counts = {}

    def ev(self, eng, emit, waits=(), dma_n=0, dsem=None):
        idx = len(self.events)
        key = dsem if dma_n else eng
        self.counts[key] = self.counts.get(key, 0) + (16 * dma_n if dma_n else 1)
        self.events.append((idx, eng, emit, dma_n, key,
                            tuple(w for w in waits if w is not None),
                            self.counts[key]))
        return idx

    def emit_engine(self, eng_name, engine, sems):
        last_wait = {}
        for idx, eng, emit, dma_n, key, waits, _ in self.events:
            if eng != eng_name:
                continue
            need = {}
            for w in waits:
                _, peng, _, pdma, pkey, _, pcnt = self.events[w]
                if pkey == eng_name and eng_name in ("T", "Y") and not pdma:
                    continue
                if need.get(pkey, -1) < pcnt:
                    need[pkey] = pcnt
            for pkey in sorted(need):
                val = need[pkey]
                if last_wait.get(pkey, -1) < val:
                    engine.wait_ge(sems[pkey], val)
                    last_wait[pkey] = val
            state = [0]
            sem = sems[key]
            if dma_n:
                def att(instr, sem=sem, state=state):
                    instr.then_inc(sem, 16)
                    state[0] += 1
                emit(engine, att)
                assert state[0] == dma_n, (eng_name, idx, dma_n, state[0])
            else:
                def att(instr, sem=sem, state=state):
                    assert state[0] == 0, (eng_name, idx)
                    instr.then_inc(sem, 1)
                    state[0] += 1
                emit(engine, att)
                assert state[0] == 1, (eng_name, idx)


def build_program(L):
    assert L % (2 * TILE) == 0
    NT = L // TILE
    NOT = NT // 2
    nc = bass.Bass("TRN2", target_bir_lowering=False,
                   dynamic_dma_scratch_size=16384,
                   detect_race_conditions=False)

    xT = nc.dram_tensor("xT", [D, L], F32, kind="ExternalInput").ap()
    xown = nc.dram_tensor("xown", [L // 2, D], F32, kind="ExternalInput").ap()
    wq_d = nc.dram_tensor("wq", [D, 2 * DK], F32, kind="ExternalInput").ap()
    wk_d = nc.dram_tensor("wk", [D, 2 * DK], F32, kind="ExternalInput").ap()
    wv_d = nc.dram_tensor("wv", [D, 2 * DV], F32, kind="ExternalInput").ap()
    wg_d = nc.dram_tensor("wg", [D, 2 * DV], F32, kind="ExternalInput").ap()
    wgk1_d = nc.dram_tensor("wgk1", [D, 16], F32, kind="ExternalInput").ap()
    wgk2_d = nc.dram_tensor("wgk2", [16, 2 * DK], F32, kind="ExternalInput").ap()
    bgk2_d = nc.dram_tensor("bgk2", [1, 2 * DK], F32, kind="ExternalInput").ap()
    wo_d = nc.dram_tensor("wo", [2 * D, D], F32, kind="ExternalInput").ap()
    gnwc_d = nc.dram_tensor("gnwc", [128, 4], F32, kind="ExternalInput").ap()
    rhs_d = nc.dram_tensor("rhs", [D, D], F32, kind="ExternalInput").ap()
    par_d = nc.dram_tensor("parity", [1, 1], I32, kind="ExternalInput").ap()
    out_d = nc.dram_tensor("out", [L // 2, D], F32, kind="ExternalOutput").ap()

    sdata = nc.dram_tensor("sdata", [2 * D, L], BF16, kind="Internal",
                           addr_space="Shared")
    sflag = nc.dram_tensor("sflag", [2, max(NT, 16)], I32, kind="Internal",
                           addr_space="Shared")

    plan = Plan()
    P = plan.ev
    _CHN["V"] = 0
    _CHN["S"] = 0
    es = contextlib.ExitStack()
    sb = lambda n, s, d: es.enter_context(nc.sbuf_tensor(n, s, d))
    ps = lambda n, s, d: es.enter_context(nc.psum_tensor(n, s, d))

    ident = sb("ident", [128, 128], BF16)
    identF = sb("identF", [128, 128], F32)
    lneg = sb("lneg", [128, 128], BF16)
    mask128 = sb("mask128", [128, 128], BF16)
    ones_col = sb("ones_col", [128, 1], BF16)
    ones_row = sb("ones_row", [1, 128], BF16)
    par_sb = sb("par_sb", [1, 1], I32)
    magic_sb = sb("magic_sb", [1, 16], I32)
    flagrx = sb("flagrx", [1, 16], I32)
    gnwc_sb = sb("gnwc_sb", [128, 4], F32)

    wq = sb("wq_bf", [128, 8 * 2 * DK], BF16)
    wk = sb("wk_bf", [128, 8 * 2 * DK], BF16)
    wv = sb("wv_bf", [128, 8 * 2 * DV], BF16)
    wg = sb("wg_bf", [128, 8 * 2 * DV], BF16)
    wgk1 = sb("wgk1_bf", [128, 8 * 16], BF16)
    wgk2h = sb("wgk2h_bf", [16, 2 * DK], BF16)
    bgk2b = sb("bgk2_bf", [1, 2 * DK], BF16)
    qrhs = sb("qrhs_bf", [128, 8 * D], BF16)
    mb_sb = sb("mb_sb", [128, 512], BF16)
    rs_bcast = sb("rs_bcast", [128, D], F32)

    ptA = ps("ptA", [128, 512], BF16)
    ptB = ps("ptB", [128, 512], BF16)
    P0 = ps("P0", [128, 512], F32)
    P1 = ps("P1", [128, 512], F32)
    P2 = ps("P2", [128, 512], F32)
    P3 = ps("P3", [128, 512], F32)
    P4 = ps("P4", [128, 512], F32)
    P5 = ps("P5", [128, 512], F32)

    xf = sb("xf", [128, 4 * 512], F32)
    xbf = [sb(f"xbf{i}", [128, 8 * 512], BF16) for i in range(2)]
    xsq = sb("xsq", [128, 2 * 512], BF16)
    qb = sb("qb", [128, 4 * 512], BF16)
    kb = sb("kb", [128, 4 * 512], BF16)
    vb = sb("vb", [128, 4 * 1024], BF16)
    sgb = sb("sgb", [128, 4 * 1024], BF16)
    t17 = sb("t17", [16, 512], BF16)
    u_f = sb("u_f", [128, 512], F32)
    spb = [sb(f"spb{i}", [128, 512], BF16) for i in range(NTC)]
    eqb = [sb(f"eqb{i}", [128, 512], BF16) for i in range(NTC)]
    ekb = [sb(f"ekb{i}", [128, 512], BF16) for i in range(NTC)]
    eblA = sb("eblA", [128, 16], F32)
    qg = [sb(f"qg{i}", [128, 256], BF16) for i in range(2)]
    ki = [sb(f"ki{i}", [128, 256], BF16) for i in range(2)]
    ktb = [sb(f"ktb{i}", [128, 256], BF16) for i in range(2)]
    stb = [sb(f"stb{i}", [128, 128], BF16) for i in range(2)]
    ekb2 = [sb(f"ekb2_{i}", [128, 512], BF16) for i in range(NTC)]
    ki2 = [sb(f"ki2_{i}", [128, 256], BF16) for i in range(2)]
    blr = sb("blr", [128, 4], F32)
    S_sb = sb("S_sb", [128, 4 * 512], BF16)
    o_park = sb("o_park", [128, 8 * 512], BF16)
    osq = sb("osq", [128, 512], BF16)
    ogtm = sb("ogtm", [128, 512], BF16)
    ogtm2 = sb("ogtm2", [128, 512], BF16)
    ogfm = sb("ogfm", [128, 8 * 512], BF16)
    rmsA = sb("rmsA", [128, 8], F32)
    rmsB = sb("rmsB", [128, 8], F32)
    m_rowf_t = sb("m_rowf", [1, 512], F32)
    e2_row_t = sb("e2_row", [1, 512], F32)
    rstd_row_t = sb("rstd_row", [1, 512], F32)
    m_rowb_t = sb("m_rowb", [1, 512], BF16)
    rr_rowb_t = sb("rr_rowb", [1, 512], BF16)
    rhi_row_t = sb("rhi_row", [1, 512], BF16)
    rlo_row_t = sb("rlo_row", [1, 512], BF16)
    m_rowf = m_rowf_t[0:1, :]
    e2_row = e2_row_t[0:1, :]
    rstd_row = rstd_row_t[0:1, :]
    m_rowb = m_rowb_t[0:1, :]
    rr_rowb = rr_rowb_t[0:1, :]
    rhi_row = rhi_row_t[0:1, :]
    rlo_row = rlo_row_t[0:1, :]
    rstd_c = sb("rstd_c", [128, NTC], F32)
    nrstd_c = sb("nrstd_c", [128, NTC], F32)
    rstd2_c = sb("rstd2_c", [128, NTC], F32)

    xo = xf
    x1b = sb("x1b", [128, 2 * 1024], F32)
    subb = sb("subb", [128, 2 * 1024], F32)
    qib = sb("qib", [128, 1024], I32)
    qbfb = kb
    qfm = S_sb
    t1b = sb("t1b", [128, 1024], F32)
    s2a = sb("s2a", [128, 2], F32)
    ssqa = sb("ssqa", [128, 2], F32)
    m2a = sb("m2a", [128, 2], F32)
    var2a = sb("var2a", [128, 2], F32)
    rstd2a = sb("rstd2a", [128, 2], F32)
    amaxa = sb("amaxa", [128, 2], F32)
    msqa = sb("msqa", [128, 2], F32)
    lsa = sb("lsa", [128, 2], F32)
    invqa = sb("invqa", [128, 2], F32)

    wo_t = [vb, sgb, o_park, ogfm]

    def wo_slice(g, c0, c1):
        return wo_t[g // 4][:, (g % 4) * 1024 + c0:(g % 4) * 1024 + c1]

    CONSTS = [1.0 / D, EPS, -0.5, 0.5, -1.0, 1.0, -LN16, 1.0 / DV,
              MAGIC, -MAGIC, 1.0 / 127.0, LN127]
    cpool = sb("cpool", [128, len(CONSTS)], F32)
    for ci, cv in enumerate(CONSTS):
        nc.const_aps.register(F32, cv, cpool[:, ci:ci + 1])

    stg = subb
    rqT = x1b[:, 0:1024]
    rqi = qib[:, 0:1024]
    rqTq = qb[:, 0:1024]
    rs_row = subb[0:1, 0:1024]
    rs_hi = kb[0:1, 0:1024]
    rs_lo = kb[0:1, 1024:2048]
    ra_col = sb("ra_col", [128, 8], F32)
    inv_col = sb("inv_col", [128, 8], F32)
    rs_col = sb("rs_col", [128, 8], F32)
    tmp_red = sb("tmp_red", [128, 1], F32)
    ptF = P0
    ptCS = P1[0:1, 0:512]
    ptRB = P2

    def em_const_v(v, att):
        for ci, cv in enumerate(CONSTS):
            v.memset(cpool[:, ci:ci + 1], cv)
        v.memset(ones_col[:], 1.0)
        v.memset(ones_row[:], 1.0)
        v.memset(magic_sb[:], FLAG)
        v.memset(flagrx[:], 0)
        v.memset(ident[:], 1.0)
        v.memset(identF[:], 1.0)
        v.memset(mask128[:], 1.0)
        att(v.memset(lneg[:], -0.0625))
    e_vc = P("V", em_const_v)

    def em_const_g(g, att):
        g.affine_select(out=ident[:], in_=ident[:], pattern=[[1, 128]],
                        compare_op=ALU.is_equal, fill=0.0, channel_multiplier=-1)
        g.affine_select(out=identF[:], in_=identF[:], pattern=[[1, 128]],
                        compare_op=ALU.is_equal, fill=0.0, channel_multiplier=-1)
        g.affine_select(out=mask128[:], in_=mask128[:], pattern=[[1, 128]],
                        compare_op=ALU.is_ge, fill=0.0, channel_multiplier=-1)
        g.affine_select(out=lneg[:], in_=lneg[:],
                        pattern=[[1, 128]], compare_op=ALU.is_ge, fill=0.0,
                        channel_multiplier=-1)
        att(g.engine_nop())
    e_gc = P("G", em_const_g, waits=[e_vc])

    e_par = P("Y", lambda y, att: att(y.dma_start(out=par_sb[:], in_=par_d)),
              dma_n=1, dsem="mp")
    e_gnw = P("Y", lambda y, att: att(y.dma_start(out=gnwc_sb[:], in_=gnwc_d)),
              dma_n=1, dsem="mg")

    pre_xl, pre_cast = {}, {}
    for k in range(8):
        sl = k % 4
        pre_xl[k] = P("Y", lambda y, att, k=k, sl=sl: att(
            y.dma_start(out=xf[:, sl * 512:(sl + 1) * 512],
                        in_=xT[k * 128:(k + 1) * 128, 0:TILE])),
            dma_n=1, dsem=f"sx{sl}",
            waits=[pre_cast[k - 4]] if k >= 4 else [])
        pre_cast[k] = P("S", lambda s, att, k=k, sl=sl: att(
            s.copy(out=xbf[0][:, k * 512:(k + 1) * 512],
                   in_=xf[:, sl * 512:(sl + 1) * 512])),
            waits=[pre_xl[k]])

    slot_free = [None, None, None]
    slot_i = [0]
    stg_slots = [stg[:, 0:1024], stg[:, 1024:2048], x1b[:, 1024:2048]]
    stg_sems = ["st0", "st1", "rq0"]

    def stage(src, dst, scale_col=None, extra_waits=()):
        sl = slot_i[0] % 3
        slot_i[0] += 1
        w = src.shape[1]
        e_l = P("Y", lambda y, att, src=src, sl=sl, w=w: att(
            y.dma_start(out=stg_slots[sl][:, 0:w], in_=src)),
            dma_n=1, dsem=stg_sems[sl], waits=[slot_free[sl]])
        if scale_col is None:
            e_c = P("S", lambda s, att, dst=dst, sl=sl, w=w: att(
                s.copy(out=dst, in_=stg_slots[sl][:, 0:w])),
                waits=[e_l] + list(extra_waits))
        else:
            e_c = P("V", lambda v, att, dst=dst, sl=sl, w=w, sc=scale_col: att(
                v.tensor_scalar(out=dst, in0=stg_slots[sl][:, 0:w],
                                scalar1=sc, scalar2=None, op0=ALU.mult)),
                waits=[e_l] + list(extra_waits))
        slot_free[sl] = e_c
        return e_c

    ev_wq = [stage(wq_d[k * 128:(k + 1) * 128, :], wq[:, k * 512:(k + 1) * 512])
             for k in range(8)]
    ev_wk = [stage(wk_d[k * 128:(k + 1) * 128, :], wk[:, k * 512:(k + 1) * 512])
             for k in range(8)]
    ev_wv = [stage(wv_d[k * 128:(k + 1) * 128, :], wv[:, k * 1024:(k + 1) * 1024])
             for k in range(8)]
    ev_wg = [stage(wg_d[k * 128:(k + 1) * 128, :], wg[:, k * 1024:(k + 1) * 1024])
             for k in range(8)]
    ev_w1 = [stage(wgk1_d[k * 128:(k + 1) * 128, :], wgk1[:, k * 16:(k + 1) * 16])
             for k in range(8)]

    e_l2 = P("Y", lambda y, att: att(
        y.dma_start(out=stg[0:16, 0:2 * DK], in_=wgk2_d)),
        dma_n=1, dsem="st0", waits=[slot_free[0]])
    e_lb = P("Y", lambda y, att: att(
        y.dma_start(out=stg[0:1, 1024:1024 + 2 * DK], in_=bgk2_d)), dma_n=1,
        dsem="st1", waits=[slot_free[1]])
    e_wgk2 = P("S", lambda s, att: att(
        s.copy(out=wgk2h[:], in_=stg[0:16, 0:2 * DK])), waits=[e_l2])
    e_bgk2 = P("S", lambda s, att: att(
        s.copy(out=bgk2b[:], in_=stg[0:1, 1024:1024 + 2 * DK])), waits=[e_lb])
    slot_free[0] = e_wgk2
    slot_free[1] = e_bgk2

    prev_cs = [None]

    rslot_free = [slot_free[0]] * 4
    prev_trF = [None]
    prev_trB = [None]
    e_qrhs_last = [None]
    for c in range(8):
        ev_blocks = []
        for f in range(8):
            sl = f % 4
            e_l = P("Y", lambda y, att, f=f, c=c, sl=sl: att(
                y.dma_start(out=stg[:, sl * 128:sl * 128 + 128],
                            in_=rhs_d[f * 128:(f + 1) * 128,
                                      c * 128:(c + 1) * 128])),
                dma_n=1, dsem=f"rq{sl}", waits=[rslot_free[sl]])
            e_t = P("T", lambda t, att, sl=sl: att(
                t.transpose(ptF[:, 0:128], stg[:, sl * 128:sl * 128 + 128],
                            identF[:])),
                waits=[e_l, e_gc, prev_trF[0]])
            e_e = P("V", lambda v, att, f=f: att(
                v.tensor_copy(out=rqT[:, f * 128:(f + 1) * 128],
                              in_=ptF[:, 0:128])), waits=[e_t])
            prev_trF[0] = e_e
            rslot_free[sl] = e_e
            ev_blocks.append(e_e)

        def em_amax(v, att, c=c):
            _chain("V", v, v.tensor_reduce(out=ra_col[:, c:c + 1],
                                           in_=rqT[:, 0:128], axis=AX.X,
                                           op=ALU.max, apply_absolute_value=True))
            i = None
            for f in range(1, 8):
                _chain("V", v, v.tensor_reduce(
                    out=tmp_red[:], in_=rqT[:, f * 128:(f + 1) * 128],
                    axis=AX.X, op=ALU.max, apply_absolute_value=True))
                _chain("V", v, v.tensor_tensor(
                    out=ra_col[:, c:c + 1], in0=ra_col[:, c:c + 1],
                    in1=tmp_red[:], op=ALU.max))
                i = True
            att(v.engine_nop())
        e_am = P("V", em_amax, waits=ev_blocks)

        def em_inv(v, att, c=c):
            _chain("V", v, v.tensor_scalar_max(out=ra_col[:, c:c + 1],
                                               in0=ra_col[:, c:c + 1],
                                               scalar1=1e-12))
            _chain("V", v, v.reciprocal(out=inv_col[:, c:c + 1],
                                        in_=ra_col[:, c:c + 1]))
            v.tensor_scalar(out=rs_col[:, c:c + 1], in0=ra_col[:, c:c + 1],
                            scalar1=1.0 / 127.0, scalar2=None, op0=ALU.mult)
            att(v.tensor_scalar(out=inv_col[:, c:c + 1], in0=inv_col[:, c:c + 1],
                                scalar1=127.0, scalar2=None, op0=ALU.mult))
        e_iv = P("V", em_inv, waits=[e_am])

        e_sc = P("V", lambda v, att, c=c: att(
            v.tensor_scalar(out=rqT, in0=rqT, scalar1=inv_col[:, c:c + 1],
                            scalar2=None, op0=ALU.mult)), waits=[e_iv])

        def em_round(s, att):
            s.activation(out=rqT, in_=rqT, func=AF.Copy, bias=MAGIC)
            s.activation(out=rqi, in_=rqT, func=AF.Copy, bias=-MAGIC)
            att(s.copy(out=rqTq, in_=rqi))
        e_rd = P("S", em_round, waits=[e_sc])

        for f in range(8):
            e_t = P("T", lambda t, att, f=f: att(
                t.transpose(ptA[:, 0:128], rqTq[:, f * 128:(f + 1) * 128],
                            ident[:])), waits=[e_rd, prev_trB[0], e_gc])
            e_e = P("V", lambda v, att, f=f, c=c: att(
                v.tensor_copy(out=qrhs[:, f * D + c * 128:f * D + (c + 1) * 128],
                              in_=ptA[:, 0:128])), waits=[e_t])
            prev_trB[0] = e_e
            e_qrhs_last[0] = e_e

    ev_rsrow = []
    prev = [prev_trF[0]]
    for c in range(8):
        e_t = P("T", lambda t, att, c=c: att(
            t.transpose(ptF[0:1, 0:128], rs_col[:, c:c + 1], identF[:])),
            waits=[prev[0], e_qrhs_last[0]])
        e_e = P("V", lambda v, att, c=c: att(
            v.tensor_copy(out=rs_row[0:1, c * 128:(c + 1) * 128],
                          in_=ptF[0:1, 0:128])), waits=[e_t])
        prev[0] = e_e
        ev_rsrow.append(e_e)

    def em_hilo(v, att):
        _chain("V", v, v.tensor_copy(out=rs_hi, in_=rs_row))
        att(v.tensor_sub(rs_lo, rs_row, rs_hi))
    e_hilo = P("V", em_hilo, waits=ev_rsrow)

    prev_rb = [None]
    for h0 in (0, 512):
        def em_mm(t, att, h0=h0):
            t.matmul(ptRB[:, 0:512], ones_row[:], rs_hi[0:1, h0:h0 + 512],
                     start=True, stop=False)
            att(t.matmul(ptRB[:, 0:512], ones_row[:], rs_lo[0:1, h0:h0 + 512],
                         start=False, stop=True))
        e_mm = P("T", em_mm, waits=[e_hilo, e_vc, prev_rb[0]])
        e_e = P("V", lambda v, att, h0=h0: att(
            v.tensor_copy(out=rs_bcast[:, h0:h0 + 512], in_=ptRB[:, 0:512])),
            waits=[e_mm])
        prev_rb[0] = e_e
    e_rsb_done = prev_rb[0]

    SETUP_STG_DONE = [slot_free[0], slot_free[1]] + list(rslot_free) + [ev_rsrow[-1], e_rsb_done]

    e_sinit = P("V", lambda v, att: att(v.memset(S_sb[:], 0.0)))

    ev_cast = {}
    ev_lastx = [None, None]
    prev_sq_mm = [None, None]
    prev_T_last = [None]
    prev_oev15 = [None]
    prev_o = [None, None]
    prev_supd = [None, None]
    prev_supd15 = [None]
    prev_mask = [None, None]
    prev_kt = [None, None]
    prev_ogev_last = [e_qrhs_last[0]]
    prev_ogmul = {}
    prev_ogt = [None, None]
    prev_rmsb7 = [None]
    pt_ev = {0: None, 1: None}
    prev_slab = [None]
    prev_bmm = {}
    prev_eq_reader = {}
    g_base = [plan.counts["G"]]

    WQK0 = ev_wq + ev_wk + [e_wgk2, e_bgk2, e_rsb_done, e_qrhs_last[0],
                            e_sinit, e_par]

    for tau in range(NT):
        T0 = tau * TILE
        buf = tau % 2
        xb = xbf[buf]

        for k in range(8):
            gidx = tau * 8 + k
            sl = gidx % 4
            if tau == 0:
                ev_cast[gidx] = pre_cast[k]
                continue
            e_l = P("Y", lambda y, att, k=k, sl=sl, T0=T0: att(
                y.dma_start(out=xf[:, sl * 512:(sl + 1) * 512],
                            in_=xT[k * 128:(k + 1) * 128, T0:T0 + TILE])),
                dma_n=1, dsem=f"sx{sl}",
                waits=[ev_cast.get(gidx - 4)])
            ev_cast[gidx] = P("S", lambda s, att, k=k, sl=sl, xb=xb: att(
                s.copy(out=xb[:, k * 512:(k + 1) * 512],
                       in_=xf[:, sl * 512:(sl + 1) * 512])),
                waits=[e_l, ev_lastx[buf]])
        e_casts = ev_cast[tau * 8 + 7]

        def em_sum(t, att, xb=xb):
            i = None
            for k in range(8):
                i = t.matmul(P2[0:1, 0:512], ones_col[:],
                             xb[:, k * 512:(k + 1) * 512],
                             start=(k == 0), stop=(k == 7))
            att(i)
        e_msum = P("T", em_sum,
                   waits=[e_casts, e_vc, prev_supd15[0]] +
                         (WQK0 if tau == 0 else [prev_T_last[0]]))

        e_mssq = None
        for k in range(8):
            e_square = P("S", lambda s, att, k=k, xb=xb: att(
                s.activation(out=xsq[:, (k % 2) * 512:(k % 2 + 1) * 512],
                             in_=xb[:, k * 512:(k + 1) * 512], func=AF.Square)),
                waits=[ev_cast[tau * 8 + k], prev_sq_mm[k % 2]])
            e_mssq = P("T", lambda t, att, k=k: att(
                t.matmul(P3[0:1, 0:512], ones_col[:],
                         xsq[:, (k % 2) * 512:(k % 2 + 1) * 512],
                         start=(k == 0), stop=(k == 7))),
                waits=[e_square] + ([e_msum] if k == 0 else []))
            prev_sq_mm[k % 2] = e_mssq

        def em_rows1(s, att):
            s.activation(out=m_rowf, in_=P2[0:1, 0:512], func=AF.Copy,
                         scale=1.0 / D)
            s.copy(out=m_rowb, in_=m_rowf)
            s.activation(out=e2_row, in_=P3[0:1, 0:512], func=AF.Copy,
                         scale=1.0 / D)
            att(s.activation(out=m_rowf, in_=m_rowf, func=AF.Square))
        e_rows1 = P("S", em_rows1, waits=[e_msum, e_mssq])

        e_var = P("V", lambda v, att: att(
            v.tensor_sub(e2_row, e2_row, m_rowf)), waits=[e_rows1])

        def em_rows2(s, att):
            s.activation(out=e2_row, in_=e2_row, func=AF.Ln, bias=EPS)
            s.activation(out=rstd_row, in_=e2_row, func=AF.Exp, scale=-0.5)
            s.activation(out=rr_rowb, in_=e2_row, func=AF.Exp, scale=0.5)
            att(s.copy(out=rhi_row, in_=rstd_row))
        e_rows2 = P("S", em_rows2, waits=[e_var])

        e_rlo = P("V", lambda v, att: att(
            v.tensor_sub(rlo_row, rstd_row, rhi_row[:])), waits=[e_rows2])

        def em_cols(t, att):
            i = None
            for tc in range(NTC):
                t.matmul(P2[:, 256 + tc:257 + tc],
                         rhi_row_t[0:1, tc * 128:(tc + 1) * 128],
                         ones_col[0:1, 0:1], start=True, stop=False)
                i = t.matmul(P2[:, 256 + tc:257 + tc],
                             rlo_row_t[0:1, tc * 128:(tc + 1) * 128],
                             ones_col[0:1, 0:1], start=False, stop=True)
            att(i)
        e_cols = P("T", em_cols, waits=[e_rlo, e_rows1])

        def em_colsS(s, att):
            s.copy(out=rstd_c[:], in_=P2[:, 256:256 + NTC])
            s.activation(out=nrstd_c[:], in_=rstd_c[:], func=AF.Copy, scale=-1.0)
            att(s.activation(out=rstd2_c[:], in_=rstd_c[:], func=AF.Square))
        e_colsS = P("S", em_colsS, waits=[e_cols])

        def em_mb(t, att):
            att(t.matmul(P3[:, 0:512], ones_row[:], m_rowb,
                         start=True, stop=True))
        e_mb = P("T", em_mb, waits=[e_rows1])
        e_mbev = P("V", lambda v, att: att(
            v.tensor_copy(out=mb_sb[:], in_=P3[:, 0:512])), waits=[e_mb])

        def em_xsub(v, att, xb=xb):
            i = None
            for k in range(8):
                i = v.tensor_sub(xb[:, k * 512:(k + 1) * 512],
                                 xb[:, k * 512:(k + 1) * 512], mb_sb[:])
            att(i)
        e_xsub = P("V", em_xsub,
                   waits=[e_mbev, prev_sq_mm[0], prev_sq_mm[1]])

        evac_evs = []
        gi = [0]

        def add_fm_group(wsb, dstsb, m, extra_evac_waits):
            bank = [P0, P1][gi[0] % 2]
            bank_wait = evac_evs[gi[0] - 2] if gi[0] >= 2 else prev_mask[gi[0]]

            def em_mm(t, att, wsb=wsb, m=m, bank=bank, xb=xb):
                i = None
                for k in range(8):
                    i = t.matmul(bank[:, 0:512],
                                 wsb[:, k * 512 + m * 128:k * 512 + (m + 1) * 128],
                                 xb[:, k * 512:(k + 1) * 512],
                                 start=(k == 0), stop=(k == 7))
                att(i)
            e_mm = P("T", em_mm, waits=[e_xsub, bank_wait])

            def em_ev(v, att, dstsb=dstsb, m=m, bank=bank):
                att(v.tensor_copy(out=dstsb[:, m * 512:(m + 1) * 512],
                                  in_=bank[:, 0:512]))
            e_ev = P("V", em_ev, waits=[e_mm] + extra_evac_waits)
            evac_evs.append(e_ev)
            gi[0] += 1
            return e_ev

        def add_tm_group(wsb, dstsb, tc, half, kind, extra_evac_waits):
            bank = [P0, P1][gi[0] % 2]
            bank_wait = evac_evs[gi[0] - 2] if gi[0] >= 2 else prev_mask[gi[0]]

            def em_mm(t, att, wsb=wsb, tc=tc, half=half, bank=bank,
                      xb=xb):
                i = None
                for k in range(8):
                    i = t.matmul(bank[:, 0:512],
                                 xb[:, k * 512 + tc * 128:k * 512 + (tc + 1) * 128],
                                 wsb[:, k * 1024 + half * 512:k * 1024 + (half + 1) * 512],
                                 start=(k == 0), stop=(k == 7))
                att(i)
            e_mm = P("T", em_mm, waits=[e_xsub, bank_wait])

            if kind == "v":
                def em_ev(v, att, dstsb=dstsb, tc=tc, half=half, bank=bank):
                    att(v.tensor_scalar(
                        out=dstsb[:, tc * 1024 + half * 512:
                                  tc * 1024 + (half + 1) * 512],
                        in0=bank[:, 0:512], scalar1=rstd2_c[:, tc:tc + 1],
                        scalar2=None, op0=ALU.mult))
                e_ev = P("V", em_ev, waits=[e_mm, e_colsS] + extra_evac_waits)
            else:
                def em_ev(s, att, dstsb=dstsb, tc=tc, half=half, bank=bank):
                    att(s.activation(
                        out=dstsb[:, tc * 1024 + half * 512:
                                  tc * 1024 + (half + 1) * 512],
                        in_=bank[:, 0:512], func=AF.Silu,
                        scale=rstd_c[:, tc:tc + 1]))
                e_ev = P("S", em_ev, waits=[e_mm, e_colsS] + extra_evac_waits)
            evac_evs.append(e_ev)
            gi[0] += 1
            return e_ev

        qk_extra = [prev_o[0], prev_o[1]]
        qk_extra = [prev_eq_reader.get(3)]
        for m in range(4):
            add_fm_group(wq, qb, m, qk_extra)
        for m in range(4):
            add_fm_group(wk, kb, m, qk_extra)

        def em_t17(t, att, xb=xb):
            i = None
            for k in range(8):
                i = t.matmul(P5[0:16, 0:512], wgk1[:, k * 16:(k + 1) * 16],
                             xb[:, k * 512:(k + 1) * 512],
                             start=(k == 0), stop=(k == 7))
            att(i)
        e_t17 = P("T", em_t17, waits=[e_xsub, prev_oev15[0]])

        e_t17ev = P("V", lambda v, att: att(
            v.tensor_copy(out=t17[:], in_=P5[0:16, 0:512])),
            waits=[e_t17, e_rows2])

        ev_eq = {}
        ev_bmm = {}
        v_list = [(tc, h) for tc in range(NTC) for h in range(2)]
        g_list = list(v_list)
        vi = 0
        gk_step = 0
        order = []
        while vi < len(v_list) or gk_step < 2 * NTC:
            for _ in range(2):
                if vi < len(v_list):
                    order.append(("v", v_list[vi]))
                    vi += 1
            if gk_step < 2 * NTC:
                order.append(("gk", gk_step))
                gk_step += 1

        for kind, info in order:
            if kind == "v":
                tc, h = info
                add_tm_group(wv, vb, tc, h, "v", [prev_T_last[0]])
            else:
                step = info
                tc = step // 2
                if step % 2 == 0:
                    def em_gkmm(t, att, tc=tc):
                        t.matmul(P4[:, 0:512],
                                 t17[0:16, tc * 128:(tc + 1) * 128],
                                 wgk2h[:], start=True, stop=False)
                        att(t.matmul(P4[:, 0:512],
                                     rr_rowb_t[0:1, tc * 128:(tc + 1) * 128],
                                     bgk2b[:], start=False, stop=True))
                    e_gkmm = P("T", em_gkmm,
                               waits=[e_t17ev,
                                      ev_eq.get(tc - 1) or prev_eq_reader.get("eq3")])

                    def em_usp(s, att, tc=tc):
                        s.activation(out=u_f[:], in_=P4[:, 0:512], func=AF.Exp,
                                     scale=nrstd_c[:, tc:tc + 1])
                        att(s.activation(out=spb[tc][:], in_=u_f[:],
                                         func=AF.Ln, bias=1.0))
                    globals()["_e_usp%d" % tc] = P(
                        "S", em_usp,
                        waits=[e_gkmm, e_colsS, prev_bmm.get(tc)])
                else:
                    e_usp = globals()["_e_usp%d" % tc]

                    def em_bmm(t, att, tc=tc):
                        i = None
                        for cst in range(4):
                            i = t.matmul(P4[:, cst * 128:(cst + 1) * 128],
                                         spb[tc][:, cst * 128:(cst + 1) * 128],
                                         lneg[:], start=True, stop=True)
                        att(i)
                    e_bmm = P("T", em_bmm, waits=[e_usp, e_gc])
                    ev_bmm[tc] = e_bmm

                    def em_eq(s, att, tc=tc):
                        s.activation(out=eqb[tc][:], in_=P4[:, 0:512],
                                     func=AF.Exp, bias=-LN16)
                        s.activation(out=ekb[tc][:], in_=P4[:, 0:512],
                                     func=AF.Exp, scale=-1.0)
                        for h in range(2):
                            idx = tc * 2 + h
                            off = 2 * h * 128 + 127
                            s.activation(
                                out=eblA[:, idx * 2:idx * 2 + 2],
                                in_=P4[:, off:off + 129:128],
                                func=AF.Exp)
                        _chain("S", s, s.copy(out=blr[:, 0:4],
                                              in_=P4[:, 127:512:128]))
                        i = None
                        for sl4 in range(4):
                            i = s.activation(
                                out=ekb2[tc][:, sl4 * 128:(sl4 + 1) * 128],
                                in_=P4[:, sl4 * 128:(sl4 + 1) * 128],
                                func=AF.Exp, scale=-1.0,
                                bias=blr[:, sl4:sl4 + 1])
                        att(i)
                    ev_eq[tc] = P("S", em_eq,
                                  waits=[e_bmm, prev_eq_reader.get(tc),
                                         prev_supd15[0]])

        for tc, h in g_list:
            add_tm_group(wg, sgb, tc, h, "g",
                         [prev_ogmul.get(tc * 2 + h),
                          prev_T_last[0]])

        e_vg_last = evac_evs[-1]
        ev_lastx[buf] = evac_evs[-1]
        ev_lastx[buf] = None

        last_qg = {}
        last_ki = {}
        last_mask = {}
        last_ktev = {}
        last_kt = {}
        last_ki2 = {}
        last_supd = {}
        last_oev = {}
        last_o = {}
        last_rmsb = {}
        last_ogmul = {}
        last_ogt = {}
        ogev_tile = []

        def emit_ogt(j):
            tcj, hj = divmod(j, 2)
            ob = [ogtm, ogtm2][j % 2]
            e_og_j = last_ogmul[j]

            def em_ogx(y, att, tcj=tcj, hj=hj, ob=ob):
                for dvs in range(4):
                    dchunk = hj * 4 + dvs
                    att(y.dma_start_transpose(
                        out=ogfm[:, dchunk * 512 + tcj * 128:
                                 dchunk * 512 + (tcj + 1) * 128],
                        in_=ob[:, dvs * 128:(dvs + 1) * 128]))
            e_ogx = P("Y", em_ogx, dma_n=4, dsem="ox",
                      waits=[e_og_j,
                             prev_slab[0] if j == 0 else None])
            ogev_tile.append(e_ogx)
            last_ogt[j] = e_ogx
        for i in range(8):
            tc, h = divmod(i, 2)
            pp = i % 2
            blk = i
            eq_ev = ev_eq[tc]

            def em_qg(v, att, tc=tc, h=h, pp=pp):
                ii = None
                for half in range(2):
                    base = (2 * h + half) * 512 + tc * 128
                    ii = v.tensor_mul(
                        qg[pp][:, half * 128:(half + 1) * 128],
                        qb[:, base:base + 128],
                        eqb[tc][:, (2 * h + half) * 128:
                                (2 * h + half + 1) * 128])
                att(ii)
            e_qg = P("V", em_qg,
                     waits=[eq_ev, last_o.get(i - 2) or prev_o[i % 2]])

            def em_ki(v, att, tc=tc, h=h, pp=pp):
                ii = None
                for half in range(2):
                    base = (2 * h + half) * 512 + tc * 128
                    ii = v.tensor_mul(
                        ki[pp][:, half * 128:(half + 1) * 128],
                        kb[:, base:base + 128],
                        ekb[tc][:, (2 * h + half) * 128:
                                (2 * h + half + 1) * 128])
                att(ii)
            e_ki = P("V", em_ki,
                     waits=[eq_ev, last_o.get(i - 2) or prev_o[i % 2]])

            def em_ki2(v, att, tc=tc, h=h, pp=pp):
                ii = None
                for half in range(2):
                    base = (2 * h + half) * 512 + tc * 128
                    ii = v.tensor_mul(
                        ki2[pp][:, half * 128:(half + 1) * 128],
                        kb[:, base:base + 128],
                        ekb2[tc][:, (2 * h + half) * 128:
                                 (2 * h + half + 1) * 128])
                att(ii)
            e_ki2 = P("V", em_ki2,
                      waits=[eq_ev, last_kt.get(i - 2) or prev_kt[i % 2]])
            last_qg[i] = e_qg
            last_ki[i] = e_ki
            last_ki2[i] = e_ki2

            sc_bank = [P0, P1][i % 2]

            def em_sc(t, att, pp=pp, sc_bank=sc_bank):
                t.matmul(sc_bank[:, 0:128],
                         ki[pp][:, 0:128], qg[pp][:, 0:128],
                         start=True, stop=False)
                att(t.matmul(sc_bank[:, 0:128],
                             ki[pp][:, 128:256], qg[pp][:, 128:256],
                             start=False, stop=True))
            e_sc = P("T", em_sc,
                     waits=[e_qg, e_ki, e_vg_last,
                            last_mask.get(i - 2) or prev_mask[i % 2]])

            def em_mask(v, att, pp=pp, sc_bank=sc_bank):
                att(v.tensor_mul(stb[pp][:], sc_bank[:, 0:128], mask128[:]))
            e_mask = P("V", em_mask, waits=[e_sc])
            last_mask[i] = e_mask

            kt_bank = [ptA, ptB][i % 2]

            def em_kt(t, att, pp=pp, kt_bank=kt_bank):
                t.transpose(kt_bank[:, 0:128], ki2[pp][:, 0:128], ident[:])
                att(t.transpose(kt_bank[:, 128:256], ki2[pp][:, 128:256],
                                ident[:]))
            e_kt = P("T", em_kt,
                     waits=[e_ki2, pt_ev[i % 2] or prev_slab[0] or
                            e_qrhs_last[0]])
            last_kt[i] = e_kt

            def em_ktev(v, att, pp=pp, kt_bank=kt_bank):
                att(v.tensor_copy(out=ktb[pp][:], in_=kt_bank[:, 0:256]))
            e_ktev = P("V", em_ktev, waits=[e_kt])
            last_ktev[i] = e_ktev
            pt_ev[i % 2] = e_ktev

            def em_ds(t, att, tc=tc, h=h, pp=pp):
                t.matmul(P2[:, 0:512], ktb[pp][:, 0:128],
                         vb[:, tc * 1024 + h * 512:tc * 1024 + (h + 1) * 512],
                         start=True, stop=True)
                att(t.matmul(P3[:, 0:512], ktb[pp][:, 128:256],
                             vb[:, tc * 1024 + h * 512:tc * 1024 + (h + 1) * 512],
                             start=True, stop=True))
            e_ds = P("T", em_ds,
                     waits=[e_ktev, e_colsS,
                            last_supd.get(i - 1) or prev_supd15[0]])

            def em_o(t, att, tc=tc, h=h, pp=pp):
                orow = P5[:, 0:512]
                t.matmul(orow, stb[pp][:],
                         vb[:, tc * 1024 + h * 512:tc * 1024 + (h + 1) * 512],
                         start=True, stop=False)
                t.matmul(orow, qg[pp][:, 0:128],
                         S_sb[:, (2 * h) * 512:(2 * h + 1) * 512],
                         start=False, stop=False)
                att(t.matmul(orow, qg[pp][:, 128:256],
                             S_sb[:, (2 * h + 1) * 512:(2 * h + 2) * 512],
                             start=False, stop=True))
            e_o = P("T", em_o,
                    waits=[e_mask,
                           last_supd.get(i - 2) or prev_supd[i % 2],
                           last_oev.get(i - 1) or prev_oev15[0],
                           e_t17ev if i == 0 else None])
            last_o[i] = e_o

            def em_oev(s, att, tc=tc, h=h):
                blk = tc * 2 + h
                s.activation(out=o_park[:, blk * 512:(blk + 1) * 512],
                             in_=P5[:, 0:512], func=AF.Copy,
                             scale=rstd_c[:, tc:tc + 1])
                att(s.activation(
                    out=osq[:],
                    in_=o_park[:, blk * 512:(blk + 1) * 512],
                    func=AF.Square,
                    accum_out=rmsA[:, blk:blk + 1]))
            e_oev = P("S", em_oev, waits=[e_o, prev_ogmul.get(i)])
            last_oev[i] = e_oev

            def em_supd(v, att, h=h, pp=pp, i=i):
                ii = None
                for half, bank in ((0, P2), (1, P3)):
                    sl = (2 * h + half)
                    ii = v.scalar_tensor_tensor(
                        out=S_sb[:, sl * 512:(sl + 1) * 512],
                        in0=S_sb[:, sl * 512:(sl + 1) * 512],
                        scalar=eblA[:, i * 2 + half:i * 2 + half + 1],
                        in1=bank[:, 0:512],
                        op0=ALU.mult, op1=ALU.add)
                att(ii)
            e_supd = P("V", em_supd, waits=[e_ds, e_o])
            last_supd[i] = e_supd

        for j in range(8):
            tcj, hj = divmod(j, 2)

            def em_rmsb(s, att, blk=j):
                s.activation(out=rmsB[:, blk:blk + 1],
                             in_=rmsA[:, blk:blk + 1],
                             func=AF.Ln, scale=1.0 / DV, bias=EPS)
                att(s.activation(out=rmsB[:, blk:blk + 1],
                                 in_=rmsB[:, blk:blk + 1],
                                 func=AF.Exp, scale=-0.5))
            e_rmsb = P("S", em_rmsb, waits=[last_oev[j]])
            last_rmsb[j] = e_rmsb

            ob = [ogtm, ogtm2][j % 2]

            def em_og(v, att, tcj=tcj, hj=hj, blk=j, ob=ob):
                att(v.scalar_tensor_tensor(
                    out=ob[:], in0=o_park[:, blk * 512:(blk + 1) * 512],
                    scalar=rmsB[:, blk:blk + 1],
                    in1=sgb[:, tcj * 1024 + hj * 512:
                            tcj * 1024 + (hj + 1) * 512],
                    op0=ALU.mult, op1=ALU.mult))
            e_og = P("V", em_og,
                     waits=[e_rmsb, evac_evs[len(evac_evs) - 8 + j],
                            last_ogt.get(j - 2) or prev_ogt[j % 2]])
            last_ogmul[j] = e_og
            emit_ogt(j)

        gbase = plan.counts.get("gw", 0)

        def em_slab(g, att, tau=tau, T0=T0, gbase=gbase):
            with g.register(f"p{tau}") as p_reg:
                g.reg_load(p_reg, par_sb[0:1, 0:1])
                for side in range(2):
                    ctx = g.If_eq(p_reg, 0) if side == 0 else g.Else()
                    with ctx:
                        base = side * D
                        for d in range(8):
                            ii = g.dma_start(
                                out=sdata.ap()[base + d * 128:base + (d + 1) * 128,
                                               T0:T0 + TILE],
                                in_=ogfm[:, d * 512:(d + 1) * 512])
                            if side == 0:
                                att(ii)
                            else:
                                ii.then_inc(SEMS["gw"], 16)
                        g.wait_ge(SEMS["gw"], gbase + 128)
                        ii = g.dma_start(
                            out=sflag.ap()[side:side + 1, tau:tau + 1],
                            in_=magic_sb[0:1, 0:1])
                        if side == 0:
                            att(ii)
                        else:
                            ii.then_inc(SEMS["gw"], 16)
        e_sl = P("G", em_slab, waits=ogev_tile + [e_par, prev_slab[0]],
                 dma_n=9, dsem="gw")

        prev_T_last[0] = last_o[7]
        prev_oev15[0] = last_oev[7]
        prev_o = [last_o[6], last_o[7]]
        prev_supd = [last_supd[6], last_supd[7]]
        prev_supd15[0] = last_supd[7]
        prev_mask = [last_mask[6], last_mask[7]]
        prev_kt = [last_kt[6], last_kt[7]]
        prev_ogmul = dict(last_ogmul)
        prev_ogt = [last_ogt[6], last_ogt[7]]
        prev_rmsb7[0] = last_rmsb[7]
        prev_slab[0] = e_sl
        prev_ogev_last[0] = last_ogmul[7]
        for tc in range(NTC):
            prev_bmm[tc] = ev_bmm[tc]
            prev_eq_reader[tc] = last_ki2[tc * 2 + 1]
        prev_eq_reader["eq3"] = ev_eq[3]
        ev_lastx[buf] = last_o[7]

    ph1_T = prev_T_last[0]
    ph1_S_ev = prev_rmsb7[0]
    ph1_V_ev = prev_ogev_last[0]
    ph1_G_ev = prev_slab[0]

    A0 = P0
    A1 = P1

    wo_slot_free = [None, None]
    wo_buf_free = [ph1_T, None, None, ph1_G_ev]
    ev_wo = []
    for k in range(16):
        sl = k % 2
        e_l = P("Y", lambda y, att, k=k, sl=sl: att(
            y.dma_start(out=stg[:, sl * 1024:(sl + 1) * 1024],
                        in_=wo_d[k * 128:(k + 1) * 128, :])),
            dma_n=1, dsem=f"st{sl}",
            waits=[wo_slot_free[sl]])
        e_c = P("V", lambda v, att, k=k, sl=sl: att(
            v.tensor_scalar(out=wo_slice(k, 0, 1024),
                            in0=stg[:, sl * 1024:(sl + 1) * 1024],
                            scalar1=gnwc_sb[:, k % 4:k % 4 + 1], scalar2=None,
                            op0=ALU.mult)),
            waits=[e_l, e_gnw, wo_buf_free[k // 4]])
        wo_slot_free[sl] = e_c
        ev_wo.append(e_c)

    last_attn = {}
    last_x1 = {}
    last_t1 = {}
    last_qtrV = {}
    last_round = {}
    last_ssq2 = {}
    last_msq2 = {}
    last_rstd2 = {}
    last_sub = {}
    last_out = {}
    last_st = {}
    last_ffn = {}
    last_srd = [None]
    srd_ev = {}
    fr_xl = {}

    def AAof(ot, tc):
        return [(P0, P1), (P2, P3), (P4, P5)][(ot * NTC + tc) % 3]

    def mk_srd(ot):
        def em_srd(g, att, ot=ot):
            with g.register(f"q{ot}") as p_reg, \
                 g.register(f"f{ot}") as f_reg, \
                 g.register(f"n{ot}") as n_reg:
                g.reg_load(p_reg, par_sb[0:1, 0:1])
                for side in range(2):
                    ctx = g.If_eq(p_reg, 0) if side == 0 else g.Else()
                    with ctx:
                        gt = side * NOT + ot
                        other = 1 - side
                        g.reg_mov(f_reg, 0)
                        g.reg_alu(n_reg, f_reg, FLAG, ALU.not_equal)
                        with g.While(n_reg):
                            g.dma_start(
                                out=flagrx[0:1, 0:1],
                                in_=sflag.ap()[other:other + 1, gt:gt + 1]
                            ).then_inc(SEMS["P"], 16)
                            g.nop()
                            g.reg_load(f_reg, flagrx[0:1, 0:1])
                            g.reg_alu(n_reg, f_reg, FLAG, ALU.not_equal)
                        T0 = gt * TILE
                        for gch in range(16):
                            ii = g.dma_start(
                                out=xbf[gch // 8][:, (gch % 8) * 512:
                                                  (gch % 8 + 1) * 512],
                                in_=sdata.ap()[gch * 128:(gch + 1) * 128,
                                               T0:T0 + TILE])
                            if side == 0:
                                att(ii)
                            else:
                                ii.then_inc(SEMS["gr"], 16)
        e_srd = P("G", em_srd,
                  waits=[ph1_G_ev, ph1_T, ph1_S_ev, ph1_V_ev, last_srd[0],
                         last_attn.get((ot - 1, 3))], dma_n=16, dsem="gr")
        last_srd[0] = e_srd
        srd_ev[ot] = e_srd

    def mk_front(ot, tc):
        pb = tc % 2
        Aa, Ab = AAof(ot, tc)
        e_xl = P("Y", lambda y, att, ot=ot, tc=tc, pb=pb: att(
            y.dma_start(out=xo[:, pb * 1024:(pb + 1) * 1024],
                        in_=xown[ot * TILE + tc * 128:
                                 ot * TILE + (tc + 1) * 128, :])),
            dma_n=1, dsem=f"xo{pb}",
            waits=[ph1_T, ph1_S_ev, ph1_V_ev, ph1_G_ev, ev_wo[-1],
                   last_x1.get((ot, tc - 2)) or last_x1.get((ot - 1, tc + 2))])
        fr_xl[(ot, tc)] = e_xl

        def em_attn(t, att, tc=tc, Aa=Aa, Ab=Ab):
            i = None
            for nh, bank in ((0, Aa), (1, Ab)):
                for gch in range(16):
                    i = t.matmul(
                        bank[:, 0:512],
                        xbf[gch // 8][:, (gch % 8) * 512 + tc * 128:
                                      (gch % 8) * 512 + (tc + 1) * 128],
                        wo_slice(gch, nh * 512, (nh + 1) * 512),
                        start=(gch == 0), stop=(gch == 15))
            att(i)
        e_attn = P("T", em_attn,
                   waits=[srd_ev[ot], ph1_V_ev, ph1_S_ev, ev_wo[-1], ev_wo[-2],
                          last_t1.get(divmod(ot * NTC + tc - 3, NTC))
                          if ot * NTC + tc >= 3 else None])
        last_attn[(ot, tc)] = e_attn

    def mk_back(ot, tc):
        pb = tc % 2
        Aa, Ab = AAof(ot, tc)
        e_attn = last_attn[(ot, tc)]
        e_xl = fr_xl[(ot, tc)]
        if True:
            def em_x1(v, att, pb=pb, Aa=Aa, Ab=Ab):
                v.tensor_add(x1b[:, pb * 1024:pb * 1024 + 512], Aa[:, 0:512],
                             xo[:, pb * 1024:pb * 1024 + 512])
                att(v.tensor_add(x1b[:, pb * 1024 + 512:(pb + 1) * 1024],
                                 Ab[:, 0:512],
                                 xo[:, pb * 1024 + 512:(pb + 1) * 1024]))
            e_x1 = P("V", em_x1,
                     waits=[e_attn, e_xl,
                            last_ssq2.get((ot, tc - 2)) or
                            last_ssq2.get((ot - 1, tc + 2))])
            last_x1[(ot, tc)] = e_x1

            e_red = P("V", lambda v, att, pb=pb: att(
                v.reduce_sum(s2a[:, pb:pb + 1],
                             x1b[:, pb * 1024:(pb + 1) * 1024], axis=AX.X)),
                waits=[e_x1])

            def em_ssq2(s, att, pb=pb):
                att(s.activation(out=qfm[:, pb * 1024:(pb + 1) * 1024],
                                 in_=x1b[:, pb * 1024:(pb + 1) * 1024],
                                 func=AF.Square, accum_out=ssqa[:, pb:pb + 1]))
            e_ssq2 = P("S", em_ssq2, waits=[e_x1])
            last_ssq2[(ot, tc)] = e_ssq2

            def em_m2(v, att, pb=pb):
                v.tensor_scalar(out=m2a[:, pb:pb + 1], in0=s2a[:, pb:pb + 1],
                                scalar1=1.0 / D, scalar2=None, op0=ALU.mult)
                att(v.tensor_scalar(out=var2a[:, pb:pb + 1],
                                    in0=ssqa[:, pb:pb + 1],
                                    scalar1=1.0 / D, scalar2=None, op0=ALU.mult))
            e_m2 = P("V", em_m2,
                     waits=[e_red, e_ssq2,
                            last_msq2.get((ot, tc - 2)) or
                            last_msq2.get((ot - 1, tc + 2)),
                            last_rstd2.get((ot, tc - 2)) or
                            last_rstd2.get((ot - 1, tc + 2))])

            e_msq2 = P("S", lambda s, att, pb=pb: att(
                s.activation(out=msqa[:, pb:pb + 1], in_=m2a[:, pb:pb + 1],
                             func=AF.Square)), waits=[e_m2])
            last_msq2[(ot, tc)] = e_msq2

            e_var3 = P("V", lambda v, att, pb=pb: att(
                v.tensor_sub(var2a[:, pb:pb + 1], var2a[:, pb:pb + 1],
                             msqa[:, pb:pb + 1])), waits=[e_msq2])

            def em_rstd2(s, att, pb=pb):
                s.activation(out=var2a[:, pb:pb + 1], in_=var2a[:, pb:pb + 1],
                             func=AF.Ln, bias=EPS)
                att(s.activation(out=rstd2a[:, pb:pb + 1],
                                 in_=var2a[:, pb:pb + 1],
                                 func=AF.Exp, scale=-0.5))
            e_rstd2 = P("S", em_rstd2, waits=[e_var3])
            last_rstd2[(ot, tc)] = e_rstd2

            def em_sub(v, att, pb=pb):
                att(v.tensor_scalar(out=subb[:, pb * 1024:(pb + 1) * 1024],
                                    in0=x1b[:, pb * 1024:(pb + 1) * 1024],
                                    scalar1=m2a[:, pb:pb + 1], scalar2=None,
                                    op0=ALU.subtract))
            e_sub = P("V", em_sub,
                      waits=[e_m2, last_round.get((ot, tc - 2)) or
                             last_round.get((ot - 1, tc + 2)),
                             last_st.get((ot, tc - 2)) or
                             last_st.get((ot - 1, tc + 2))])
            last_sub[(ot, tc)] = e_sub

            def em_amax(v, att, pb=pb):
                _chain("V", v, v.tensor_reduce(
                    out=amaxa[:, pb:pb + 1],
                    in_=subb[:, pb * 1024:(pb + 1) * 1024],
                    axis=AX.X, op=ALU.max, apply_absolute_value=True))
                att(v.tensor_scalar_max(
                    out=amaxa[:, pb:pb + 1], in0=amaxa[:, pb:pb + 1],
                    scalar1=1e-30))
            e_amax = P("V", em_amax, waits=[e_sub])

            def em_invq(s, att, pb=pb):
                s.activation(out=invqa[:, pb:pb + 1], in_=amaxa[:, pb:pb + 1],
                             func=AF.Ln)
                att(s.activation(out=invqa[:, pb:pb + 1],
                                 in_=invqa[:, pb:pb + 1],
                                 func=AF.Exp, scale=-1.0, bias=LN127))
            e_invq = P("S", em_invq, waits=[e_amax])

            def em_ls(v, att, pb=pb):
                att(v.scalar_tensor_tensor(
                    out=lsa[:, pb:pb + 1], in0=amaxa[:, pb:pb + 1],
                    scalar=1.0 / 127.0, in1=rstd2a[:, pb:pb + 1],
                    op0=ALU.mult, op1=ALU.mult))
            e_ls = P("V", em_ls,
                     waits=[e_amax, e_rstd2,
                            last_t1.get((ot, tc - 2)) or
                            last_t1.get((ot - 1, tc + 2))])

            e_qm = P("V", lambda v, att, pb=pb: att(
                v.tensor_scalar(out=subb[:, pb * 1024:(pb + 1) * 1024],
                                in0=subb[:, pb * 1024:(pb + 1) * 1024],
                                scalar1=invqa[:, pb:pb + 1], scalar2=None,
                                op0=ALU.mult)), waits=[e_invq])

            def em_round(s, att, pb=pb):
                s.activation(out=subb[:, pb * 1024:(pb + 1) * 1024],
                             in_=subb[:, pb * 1024:(pb + 1) * 1024],
                             func=AF.Copy, bias=MAGIC)
                s.activation(out=qib[:], in_=subb[:, pb * 1024:(pb + 1) * 1024],
                             func=AF.Copy, bias=-MAGIC)
                att(s.copy(out=qbfb[:, pb * 1024:(pb + 1) * 1024], in_=qib[:]))
            e_round = P("S", em_round,
                        waits=[e_qm, last_qtrV.get((ot, tc - 2)) or
                               last_qtrV.get((ot - 1, tc + 2))])
            last_round[(ot, tc)] = e_round

            def em_qtr(y, att, pb=pb):
                for f in range(8):
                    att(y.dma_start_transpose(
                        out=qfm[:, pb * 1024 + f * 128:pb * 1024 + (f + 1) * 128],
                        in_=qbfb[:, pb * 1024 + f * 128:
                                 pb * 1024 + (f + 1) * 128]))
            e_qtr = P("Y", em_qtr, dma_n=8, dsem="qx",
                      waits=[e_round,
                             last_ffn.get((ot, tc - 2)) or
                             last_ffn.get((ot - 1, tc + 2))])
            qtr_evs = [e_qtr]
            last_qtrV[(ot, tc)] = e_qtr

            def em_ffn(t, att, pb=pb, Aa=Aa, Ab=Ab):
                i = None
                for nh, bank in ((0, Aa), (1, Ab)):
                    for f in range(8):
                        i = t.matmul(
                            bank[:, 0:512],
                            qfm[:, pb * 1024 + f * 128:pb * 1024 + (f + 1) * 128],
                            qrhs[:, f * D + nh * 512:f * D + (nh + 1) * 512],
                            start=(f == 0), stop=(f == 7))
                att(i)
            e_ffn = P("T", em_ffn, waits=qtr_evs + [e_x1])
            last_ffn[(ot, tc)] = e_ffn

            def em_t1(s, att, pb=pb, Aa=Aa, Ab=Ab):
                s.activation(out=t1b[:, 0:512], in_=Aa[:, 0:512], func=AF.Copy,
                             scale=lsa[:, pb:pb + 1])
                att(s.activation(out=t1b[:, 512:1024], in_=Ab[:, 0:512],
                                 func=AF.Copy, scale=lsa[:, pb:pb + 1]))
            e_t1 = P("S", em_t1,
                     waits=[e_ffn, e_ls, last_out.get((ot, tc - 1)) or
                            last_out.get((ot - 1, 3))])
            last_t1[(ot, tc)] = e_t1

            def em_out(v, att, pb=pb):
                v.tensor_mul(t1b[:], t1b[:], rs_bcast[:])
                att(v.tensor_add(subb[:, pb * 1024:(pb + 1) * 1024], t1b[:],
                                 x1b[:, pb * 1024:(pb + 1) * 1024]))
            e_out = P("V", em_out,
                      waits=[e_t1, last_st.get((ot, tc - 2)) or
                             last_st.get((ot - 1, tc + 2))])
            last_out[(ot, tc)] = e_out

            e_st = P("Y", lambda y, att, ot=ot, tc=tc, pb=pb: att(
                y.dma_start(out=out_d[ot * TILE + tc * 128:
                                      ot * TILE + (tc + 1) * 128, :],
                            in_=subb[:, pb * 1024:(pb + 1) * 1024])),
                dma_n=1, dsem=f"out{pb}", waits=[e_out])
            last_st[(ot, tc)] = e_st

    seq2 = [(o_, t_) for o_ in range(NOT) for t_ in range(NTC)]
    for idx2, (o_, t_) in enumerate(seq2):
        if idx2 >= 2:
            mk_back(*seq2[idx2 - 2])
        if t_ == 0:
            mk_srd(o_)
        mk_front(o_, t_)
    mk_back(*seq2[-2])
    mk_back(*seq2[-1])

    if _DEBUG:
        dbg_sdata = nc.dram_tensor("dbg_sdata", [2 * D, L], BF16,
                                   kind="ExternalOutput").ap()
        dbg_x1 = nc.dram_tensor("dbg_x1", [128, 2048], F32,
                                kind="ExternalOutput").ap()
        dbg_cols = nc.dram_tensor("dbg_cols", [128, 18], F32,
                                  kind="ExternalOutput").ap()
        dbg_sub = nc.dram_tensor("dbg_sub", [128, 2048], F32,
                                 kind="ExternalOutput").ap()
        dbg_rsb = nc.dram_tensor("dbg_rsb", [128, 1024], F32,
                                 kind="ExternalOutput").ap()
        dbg_qbf = nc.dram_tensor("dbg_qbf", [128, 2048], BF16,
                                 kind="ExternalOutput").ap()
        dbg_t1 = nc.dram_tensor("dbg_t1", [128, 1024], F32,
                                 kind="ExternalOutput").ap()
        dbg_qfm = nc.dram_tensor("dbg_qfm", [128, 2048], BF16,
                                 kind="ExternalOutput").ap()
        dbg_ident = nc.dram_tensor("dbg_ident", [128, 128], BF16,
                                   kind="ExternalOutput").ap()
        last_all = [last_st[(NOT - 1, tc)] for tc in range(NTC)] + \
                   [last_out[(NOT - 1, 3)], last_t1[(NOT - 1, 3)],
                    last_qtrV[(NOT - 1, 3)], ph1_G_ev]

        def em_dbg(y, att, n=[0]):
            att(y.dma_start(out=dbg_sdata, in_=sdata.ap()))
            att(y.dma_start(out=dbg_x1, in_=x1b[:]))
            att(y.dma_start(out=dbg_sub, in_=subb[:]))
            for i, t_ in enumerate((s2a, ssqa, m2a, var2a, rstd2a, amaxa,
                                    msqa, lsa, invqa)):
                att(y.dma_start(out=dbg_cols[:, i * 2:(i + 1) * 2], in_=t_[:]))
            att(y.dma_start(out=dbg_rsb, in_=rs_bcast[:]))
            att(y.dma_start(out=dbg_qbf, in_=qbfb[:]))
            att(y.dma_start(out=dbg_t1, in_=t1b[:]))
            att(y.dma_start(out=dbg_qfm, in_=qfm[:]))
            att(y.dma_start(out=dbg_ident, in_=ident[:]))
        e_dbg = P("Y", em_dbg, dma_n=17, dsem="dbg", waits=last_all)

    sem_names = (list("TSVGYP") + ["XV", "XS", "mp", "mg", "gw", "gr", "dbg"] +
                 ["qx", "ox"] + [f"st{i}" for i in range(2)] +
                 [f"rq{i}" for i in range(4)] +
                 [f"sx{i}" for i in range(4)] + ["xo0", "xo1", "out0", "out1"])
    with contextlib.ExitStack() as bs:
        for nm in sem_names:
            SEMS[nm] = bs.enter_context(nc.semaphore(f"sem_{nm}"))
        block = bs.enter_context(nc.Block())

        @block.tensor
        def _(t):
            plan.emit_engine("T", t, SEMS)

        @block.scalar
        def _(s):
            plan.emit_engine("S", s, SEMS)

        @block.vector
        def _(v):
            plan.emit_engine("V", v, SEMS)

        @block.gpsimd
        def _(g):
            plan.emit_engine("G", g, SEMS)

        @block.sync
        def _(y):
            plan.emit_engine("Y", y, SEMS)
            for p in (0, 1):
                y.wait_ge(SEMS[f"out{p}"], plan.counts.get(f"out{p}", 0))
            if _DEBUG:
                y.wait_ge(SEMS["dbg"], plan.counts.get("dbg", 0))

    es.close()
    return nc


_CACHE = {}


def _get_program(L):
    if L not in _CACHE:
        _CACHE[L] = build_program(L)
    return _CACHE[L]


def kernel(**inputs):
    x = np.ascontiguousarray(np.asarray(inputs["L0_attn_hidden_states"],
                                        dtype=np.float32))
    B, L, _ = x.shape
    Wq = np.asarray(inputs["L0_attn_Wq"], np.float32)
    Wk = np.asarray(inputs["L0_attn_Wk"], np.float32)
    Wv = np.asarray(inputs["L0_attn_Wv"], np.float32)
    Wg = np.asarray(inputs["L0_attn_Wg"], np.float32)
    Wgk1 = np.asarray(inputs["L0_attn_Wgk1"], np.float32)
    Wgk2 = np.asarray(inputs["L0_attn_Wgk2"], np.float32)
    bgk2 = np.asarray(inputs["L0_attn_bgk2"], np.float32).reshape(1, D)
    gnw = np.asarray(inputs["L0_attn_g_norm_w"], np.float32)
    Wo = np.asarray(inputs["L0_attn_Wo"], np.float32)
    rhs = np.asarray(inputs["L0_ffn_rhs"], np.float32)

    gnwc = np.ascontiguousarray(gnw.reshape(4, 128).T)
    nc = _get_program(L)

    in_maps = []
    for c in range(8):
        b, hp = divmod(c, 2)
        half = slice(hp * (L // 2), (hp + 1) * (L // 2))
        in_maps.append({
            "xT": np.ascontiguousarray(x[b].T),
            "xown": np.ascontiguousarray(x[b, half]),
            "wq": np.ascontiguousarray(Wq[:, hp * 512:(hp + 1) * 512]),
            "wk": np.ascontiguousarray(Wk[:, hp * 512:(hp + 1) * 512]),
            "wv": np.ascontiguousarray(Wv[:, hp * 1024:(hp + 1) * 1024]),
            "wg": np.ascontiguousarray(Wg[:, hp * 1024:(hp + 1) * 1024]),
            "wgk1": np.ascontiguousarray(Wgk1),
            "wgk2": np.ascontiguousarray(Wgk2[:, hp * 512:(hp + 1) * 512]),
            "bgk2": np.ascontiguousarray(bgk2[:, hp * 512:(hp + 1) * 512]),
            "wo": np.ascontiguousarray(Wo),
            "gnwc": gnwc,
            "rhs": np.ascontiguousarray(rhs),
            "parity": np.array([[hp]], dtype=np.int32),
        })
    res = run_bass_kernel_spmd(nc, in_maps, core_ids=list(range(8)))
    out = np.empty((B, L, D), np.float32)
    for c in range(8):
        b, hp = divmod(c, 2)
        out[b, hp * (L // 2):(hp + 1) * (L // 2)] = res.results[c]["out"]
    if _DEBUG:
        global _DBG_RES
        _DBG_RES = res.results
    return out


# version 19
# speedup vs baseline: 1.2481x; 1.2481x over previous
import contextlib

import numpy as np
import concourse.bass as bass
import concourse.mybir as mybir
from concourse.bass_utils import run_bass_kernel_spmd

F32 = mybir.dt.float32
BF16 = mybir.dt.bfloat16
I32 = mybir.dt.int32
AF = mybir.ActivationFunctionType
ALU = mybir.AluOpType
AX = mybir.AxisListType

D = 1024
DK = 256
DV = 512
TILE = 512
NTC = 4
MAGIC = 12582912.0
LN16 = 2.772588722239781
LN127 = 4.844187086458591
EPS = 1e-5
FLAG = 77777

SEMS = {}
_DEBUG = False
_CHN = {"V": 0, "S": 0}


def _chain(eng_key, eng, instr):
    _CHN[eng_key] += 1
    instr.then_inc(SEMS["X" + eng_key], 1)
    eng.wait_ge(SEMS["X" + eng_key], _CHN[eng_key])


class Plan:
    def __init__(self):
        self.events = []
        self.counts = {}

    def ev(self, eng, emit, waits=(), dma_n=0, dsem=None):
        idx = len(self.events)
        key = dsem if dma_n else eng
        self.counts[key] = self.counts.get(key, 0) + (16 * dma_n if dma_n else 1)
        self.events.append((idx, eng, emit, dma_n, key,
                            tuple(w for w in waits if w is not None),
                            self.counts[key]))
        return idx

    def emit_engine(self, eng_name, engine, sems):
        last_wait = {}
        for idx, eng, emit, dma_n, key, waits, _ in self.events:
            if eng != eng_name:
                continue
            need = {}
            for w in waits:
                _, peng, _, pdma, pkey, _, pcnt = self.events[w]
                if pkey == eng_name and eng_name in ("T", "Y") and not pdma:
                    continue
                if need.get(pkey, -1) < pcnt:
                    need[pkey] = pcnt
            for pkey in sorted(need):
                val = need[pkey]
                if last_wait.get(pkey, -1) < val:
                    engine.wait_ge(sems[pkey], val)
                    last_wait[pkey] = val
            state = [0]
            sem = sems[key]
            if dma_n:
                def att(instr, sem=sem, state=state):
                    instr.then_inc(sem, 16)
                    state[0] += 1
                emit(engine, att)
                assert state[0] == dma_n, (eng_name, idx, dma_n, state[0])
            else:
                def att(instr, sem=sem, state=state):
                    assert state[0] == 0, (eng_name, idx)
                    instr.then_inc(sem, 1)
                    state[0] += 1
                emit(engine, att)
                assert state[0] == 1, (eng_name, idx)


def build_program(L):
    assert L % (2 * TILE) == 0
    NT = L // TILE
    NOT = NT // 2
    nc = bass.Bass("TRN2", target_bir_lowering=False,
                   dynamic_dma_scratch_size=16384,
                   detect_race_conditions=False)

    xT = nc.dram_tensor("xT", [D, L], F32, kind="ExternalInput").ap()
    xown = nc.dram_tensor("xown", [L // 2, D], F32, kind="ExternalInput").ap()
    wq_d = nc.dram_tensor("wq", [D, 2 * DK], F32, kind="ExternalInput").ap()
    wk_d = nc.dram_tensor("wk", [D, 2 * DK], F32, kind="ExternalInput").ap()
    wv_d = nc.dram_tensor("wv", [D, 2 * DV], F32, kind="ExternalInput").ap()
    wg_d = nc.dram_tensor("wg", [D, 2 * DV], F32, kind="ExternalInput").ap()
    wgk1_d = nc.dram_tensor("wgk1", [D, 16], F32, kind="ExternalInput").ap()
    wgk2_d = nc.dram_tensor("wgk2", [16, 2 * DK], F32, kind="ExternalInput").ap()
    bgk2_d = nc.dram_tensor("bgk2", [1, 2 * DK], F32, kind="ExternalInput").ap()
    wo_d = nc.dram_tensor("wo", [2 * D, D], F32, kind="ExternalInput").ap()
    gnwc_d = nc.dram_tensor("gnwc", [128, 4], F32, kind="ExternalInput").ap()
    rhs_d = nc.dram_tensor("rhs", [D, D], F32, kind="ExternalInput").ap()
    par_d = nc.dram_tensor("parity", [1, 1], I32, kind="ExternalInput").ap()
    out_d = nc.dram_tensor("out", [L // 2, D], F32, kind="ExternalOutput").ap()

    sdata = nc.dram_tensor("sdata", [2 * D, L], BF16, kind="Internal",
                           addr_space="Shared")
    sflag = nc.dram_tensor("sflag", [2, max(NT, 16)], I32, kind="Internal",
                           addr_space="Shared")

    plan = Plan()
    P = plan.ev
    _CHN["V"] = 0
    _CHN["S"] = 0
    es = contextlib.ExitStack()
    sb = lambda n, s, d: es.enter_context(nc.sbuf_tensor(n, s, d))
    ps = lambda n, s, d: es.enter_context(nc.psum_tensor(n, s, d))

    ident = sb("ident", [128, 128], BF16)
    identF = sb("identF", [128, 128], F32)
    lneg = sb("lneg", [128, 128], BF16)
    mask128 = sb("mask128", [128, 128], BF16)
    ones_col = sb("ones_col", [128, 1], BF16)
    ones_row = sb("ones_row", [1, 128], BF16)
    par_sb = sb("par_sb", [1, 1], I32)
    magic_sb = sb("magic_sb", [1, 16], I32)
    flagrx = sb("flagrx", [1, 16], I32)
    gnwc_sb = sb("gnwc_sb", [128, 4], F32)

    wq = sb("wq_bf", [128, 8 * 2 * DK], BF16)
    wk = sb("wk_bf", [128, 8 * 2 * DK], BF16)
    wv = sb("wv_bf", [128, 8 * 2 * DV], BF16)
    wg = sb("wg_bf", [128, 8 * 2 * DV], BF16)
    wgk1 = sb("wgk1_bf", [128, 8 * 16], BF16)
    wgk2h = sb("wgk2h_bf", [16, 2 * DK], BF16)
    bgk2b = sb("bgk2_bf", [1, 2 * DK], BF16)
    qrhs = sb("qrhs_bf", [128, 8 * D], BF16)
    mb_sb = sb("mb_sb", [128, 512], BF16)
    rs_bcast = sb("rs_bcast", [128, D], F32)

    ptA = ps("ptA", [128, 512], BF16)
    ptB = ps("ptB", [128, 512], BF16)
    P0 = ps("P0", [128, 512], F32)
    P1 = ps("P1", [128, 512], F32)
    P2 = ps("P2", [128, 512], F32)
    P3 = ps("P3", [128, 512], F32)
    P4 = ps("P4", [128, 512], F32)
    P5 = ps("P5", [128, 512], F32)

    xf = sb("xf", [128, 4 * 512], F32)
    xbf = [sb(f"xbf{i}", [128, 8 * 512], BF16) for i in range(2)]
    xsq = sb("xsq", [128, 2 * 512], BF16)
    qb = sb("qb", [128, 4 * 512], BF16)
    kb = sb("kb", [128, 4 * 512], BF16)
    vb = sb("vb", [128, 4 * 1024], BF16)
    sgb = sb("sgb", [128, 4 * 1024], BF16)
    t17 = sb("t17", [16, 512], BF16)
    u_f = sb("u_f", [128, 512], F32)
    spb = [sb(f"spb{i}", [128, 512], BF16) for i in range(NTC)]
    eqb = [sb(f"eqb{i}", [128, 512], BF16) for i in range(NTC)]
    ekb = [sb(f"ekb{i}", [128, 512], BF16) for i in range(NTC)]
    eblA = sb("eblA", [128, 16], F32)
    qg = [sb(f"qg{i}", [128, 256], BF16) for i in range(2)]
    ki = [sb(f"ki{i}", [128, 256], BF16) for i in range(2)]
    ktb = [sb(f"ktb{i}", [128, 256], BF16) for i in range(2)]
    stb = [sb(f"stb{i}", [128, 128], BF16) for i in range(2)]
    ekb2 = [sb(f"ekb2_{i}", [128, 512], BF16) for i in range(NTC)]
    ki2 = [sb(f"ki2_{i}", [128, 256], BF16) for i in range(2)]
    blr = sb("blr", [128, 4], F32)
    S_sb = sb("S_sb", [128, 4 * 512], BF16)
    o_park = sb("o_park", [128, 8 * 512], BF16)
    osq = sb("osq", [128, 512], BF16)
    ogtm = sb("ogtm", [128, 512], BF16)
    ogtm2 = sb("ogtm2", [128, 512], BF16)
    ogfm = sb("ogfm", [128, 8 * 512], BF16)
    rmsA = sb("rmsA", [128, 8], F32)
    rmsB = sb("rmsB", [128, 8], F32)
    m_rowf_t = sb("m_rowf", [1, 512], F32)
    e2_row_t = sb("e2_row", [1, 512], F32)
    rstd_row_t = sb("rstd_row", [1, 512], F32)
    m_rowb_t = sb("m_rowb", [1, 512], BF16)
    rr_rowb_t = sb("rr_rowb", [1, 512], BF16)
    rhi_row_t = sb("rhi_row", [1, 512], BF16)
    rlo_row_t = sb("rlo_row", [1, 512], BF16)
    m_rowf = m_rowf_t[0:1, :]
    e2_row = e2_row_t[0:1, :]
    rstd_row = rstd_row_t[0:1, :]
    m_rowb = m_rowb_t[0:1, :]
    rr_rowb = rr_rowb_t[0:1, :]
    rhi_row = rhi_row_t[0:1, :]
    rlo_row = rlo_row_t[0:1, :]
    rstd_c = sb("rstd_c", [128, NTC], F32)
    nrstd_c = sb("nrstd_c", [128, NTC], F32)
    rstd2_c = sb("rstd2_c", [128, NTC], F32)

    xo = xf
    x1b = sb("x1b", [128, 2 * 1024], F32)
    subb = sb("subb", [128, 2 * 1024], F32)
    qib = sb("qib", [128, 1024], I32)
    qbfb = kb
    qfm = S_sb
    t1b = sb("t1b", [128, 1024], F32)
    s2a = sb("s2a", [128, 2], F32)
    ssqa = sb("ssqa", [128, 2], F32)
    m2a = sb("m2a", [128, 2], F32)
    var2a = sb("var2a", [128, 2], F32)
    rstd2a = sb("rstd2a", [128, 2], F32)
    amaxa = sb("amaxa", [128, 2], F32)
    msqa = sb("msqa", [128, 2], F32)
    lsa = sb("lsa", [128, 2], F32)
    invqa = sb("invqa", [128, 2], F32)

    wo_t = [vb, sgb, o_park, ogfm]

    def wo_slice(g, c0, c1):
        return wo_t[g // 4][:, (g % 4) * 1024 + c0:(g % 4) * 1024 + c1]

    CONSTS = [1.0 / D, EPS, -0.5, 0.5, -1.0, 1.0, -LN16, 1.0 / DV,
              MAGIC, -MAGIC, 1.0 / 127.0, LN127]
    cpool = sb("cpool", [128, len(CONSTS)], F32)
    for ci, cv in enumerate(CONSTS):
        nc.const_aps.register(F32, cv, cpool[:, ci:ci + 1])

    stg = subb
    rqT = x1b[:, 0:1024]
    rqi = qib[:, 0:1024]
    rqTq = qb[:, 0:1024]
    rs_row = subb[0:1, 0:1024]
    rs_hi = kb[0:1, 0:1024]
    rs_lo = kb[0:1, 1024:2048]
    ra_col = sb("ra_col", [128, 8], F32)
    inv_col = sb("inv_col", [128, 8], F32)
    rs_col = sb("rs_col", [128, 8], F32)
    tmp_red = sb("tmp_red", [128, 1], F32)
    ptF = P0
    ptCS = P1[0:1, 0:512]
    ptRB = P2

    def em_const_v(v, att):
        for ci, cv in enumerate(CONSTS):
            v.memset(cpool[:, ci:ci + 1], cv)
        v.memset(ones_col[:], 1.0)
        v.memset(ones_row[:], 1.0)
        v.memset(magic_sb[:], FLAG)
        v.memset(flagrx[:], 0)
        v.memset(ident[:], 1.0)
        v.memset(identF[:], 1.0)
        v.memset(mask128[:], 1.0)
        att(v.memset(lneg[:], -0.0625))
    e_vc = P("V", em_const_v)

    def em_const_g(g, att):
        g.affine_select(out=ident[:], in_=ident[:], pattern=[[1, 128]],
                        compare_op=ALU.is_equal, fill=0.0, channel_multiplier=-1)
        g.affine_select(out=identF[:], in_=identF[:], pattern=[[1, 128]],
                        compare_op=ALU.is_equal, fill=0.0, channel_multiplier=-1)
        g.affine_select(out=mask128[:], in_=mask128[:], pattern=[[1, 128]],
                        compare_op=ALU.is_ge, fill=0.0, channel_multiplier=-1)
        g.affine_select(out=lneg[:], in_=lneg[:],
                        pattern=[[1, 128]], compare_op=ALU.is_ge, fill=0.0,
                        channel_multiplier=-1)
        att(g.engine_nop())
    e_gc = P("G", em_const_g, waits=[e_vc])

    e_par = P("Y", lambda y, att: att(y.dma_start(out=par_sb[:], in_=par_d)),
              dma_n=1, dsem="mp")
    e_gnw = P("Y", lambda y, att: att(y.dma_start(out=gnwc_sb[:], in_=gnwc_d)),
              dma_n=1, dsem="mg")

    pre_xl, pre_cast = {}, {}
    for k in range(8):
        sl = k % 4
        pre_xl[k] = P("Y", lambda y, att, k=k, sl=sl: att(
            y.dma_start(out=xf[:, sl * 512:(sl + 1) * 512],
                        in_=xT[k * 128:(k + 1) * 128, 0:TILE])),
            dma_n=1, dsem=f"sx{sl}",
            waits=[pre_cast[k - 4]] if k >= 4 else [])
        pre_cast[k] = P("S", lambda s, att, k=k, sl=sl: att(
            s.copy(out=xbf[0][:, k * 512:(k + 1) * 512],
                   in_=xf[:, sl * 512:(sl + 1) * 512])),
            waits=[pre_xl[k]])

    slot_free = [None, None, None]
    slot_i = [0]
    stg_slots = [stg[:, 0:1024], stg[:, 1024:2048], x1b[:, 1024:2048]]
    stg_sems = ["st0", "st1", "rq0"]

    def stage(src, dst, scale_col=None, extra_waits=()):
        sl = slot_i[0] % 3
        slot_i[0] += 1
        w = src.shape[1]
        e_l = P("Y", lambda y, att, src=src, sl=sl, w=w: att(
            y.dma_start(out=stg_slots[sl][:, 0:w], in_=src)),
            dma_n=1, dsem=stg_sems[sl], waits=[slot_free[sl]])
        if scale_col is None:
            e_c = P("S", lambda s, att, dst=dst, sl=sl, w=w: att(
                s.copy(out=dst, in_=stg_slots[sl][:, 0:w])),
                waits=[e_l] + list(extra_waits))
        else:
            e_c = P("V", lambda v, att, dst=dst, sl=sl, w=w, sc=scale_col: att(
                v.tensor_scalar(out=dst, in0=stg_slots[sl][:, 0:w],
                                scalar1=sc, scalar2=None, op0=ALU.mult)),
                waits=[e_l] + list(extra_waits))
        slot_free[sl] = e_c
        return e_c

    ev_wq = [stage(wq_d[k * 128:(k + 1) * 128, :], wq[:, k * 512:(k + 1) * 512])
             for k in range(8)]
    ev_wk = [stage(wk_d[k * 128:(k + 1) * 128, :], wk[:, k * 512:(k + 1) * 512])
             for k in range(8)]
    ev_wv = [stage(wv_d[k * 128:(k + 1) * 128, :], wv[:, k * 1024:(k + 1) * 1024])
             for k in range(8)]
    ev_wg = [stage(wg_d[k * 128:(k + 1) * 128, :], wg[:, k * 1024:(k + 1) * 1024])
             for k in range(8)]
    ev_w1 = [stage(wgk1_d[k * 128:(k + 1) * 128, :], wgk1[:, k * 16:(k + 1) * 16])
             for k in range(8)]

    e_l2 = P("Y", lambda y, att: att(
        y.dma_start(out=stg[0:16, 0:2 * DK], in_=wgk2_d)),
        dma_n=1, dsem="st0", waits=[slot_free[0]])
    e_lb = P("Y", lambda y, att: att(
        y.dma_start(out=stg[0:1, 1024:1024 + 2 * DK], in_=bgk2_d)), dma_n=1,
        dsem="st1", waits=[slot_free[1]])
    e_wgk2 = P("S", lambda s, att: att(
        s.copy(out=wgk2h[:], in_=stg[0:16, 0:2 * DK])), waits=[e_l2])
    e_bgk2 = P("S", lambda s, att: att(
        s.copy(out=bgk2b[:], in_=stg[0:1, 1024:1024 + 2 * DK])), waits=[e_lb])
    slot_free[0] = e_wgk2
    slot_free[1] = e_bgk2

    prev_cs = [None]

    rslot_free = [slot_free[0]] * 4
    prev_trF = [None]
    prev_trB = [None]
    e_qrhs_last = [None]
    for c in range(8):
        ev_blocks = []
        for f in range(8):
            sl = f % 4
            e_l = P("Y", lambda y, att, f=f, c=c, sl=sl: att(
                y.dma_start(out=stg[:, sl * 128:sl * 128 + 128],
                            in_=rhs_d[f * 128:(f + 1) * 128,
                                      c * 128:(c + 1) * 128])),
                dma_n=1, dsem=f"rq{sl}", waits=[rslot_free[sl]])
            e_t = P("T", lambda t, att, sl=sl: att(
                t.transpose(ptF[:, 0:128], stg[:, sl * 128:sl * 128 + 128],
                            identF[:])),
                waits=[e_l, e_gc, prev_trF[0]])
            e_e = P("V", lambda v, att, f=f: att(
                v.tensor_copy(out=rqT[:, f * 128:(f + 1) * 128],
                              in_=ptF[:, 0:128])), waits=[e_t])
            prev_trF[0] = e_e
            rslot_free[sl] = e_e
            ev_blocks.append(e_e)

        def em_amax(v, att, c=c):
            _chain("V", v, v.tensor_reduce(out=ra_col[:, c:c + 1],
                                           in_=rqT[:, 0:128], axis=AX.X,
                                           op=ALU.max, apply_absolute_value=True))
            i = None
            for f in range(1, 8):
                _chain("V", v, v.tensor_reduce(
                    out=tmp_red[:], in_=rqT[:, f * 128:(f + 1) * 128],
                    axis=AX.X, op=ALU.max, apply_absolute_value=True))
                _chain("V", v, v.tensor_tensor(
                    out=ra_col[:, c:c + 1], in0=ra_col[:, c:c + 1],
                    in1=tmp_red[:], op=ALU.max))
                i = True
            att(v.engine_nop())
        e_am = P("V", em_amax, waits=ev_blocks)

        def em_inv(v, att, c=c):
            _chain("V", v, v.tensor_scalar_max(out=ra_col[:, c:c + 1],
                                               in0=ra_col[:, c:c + 1],
                                               scalar1=1e-12))
            _chain("V", v, v.reciprocal(out=inv_col[:, c:c + 1],
                                        in_=ra_col[:, c:c + 1]))
            v.tensor_scalar(out=rs_col[:, c:c + 1], in0=ra_col[:, c:c + 1],
                            scalar1=1.0 / 127.0, scalar2=None, op0=ALU.mult)
            att(v.tensor_scalar(out=inv_col[:, c:c + 1], in0=inv_col[:, c:c + 1],
                                scalar1=127.0, scalar2=None, op0=ALU.mult))
        e_iv = P("V", em_inv, waits=[e_am])

        e_sc = P("V", lambda v, att, c=c: att(
            v.tensor_scalar(out=rqT, in0=rqT, scalar1=inv_col[:, c:c + 1],
                            scalar2=None, op0=ALU.mult)), waits=[e_iv])

        def em_round(s, att):
            s.activation(out=rqT, in_=rqT, func=AF.Copy, bias=MAGIC)
            s.activation(out=rqi, in_=rqT, func=AF.Copy, bias=-MAGIC)
            att(s.copy(out=rqTq, in_=rqi))
        e_rd = P("S", em_round, waits=[e_sc])

        for f in range(8):
            e_t = P("T", lambda t, att, f=f: att(
                t.transpose(ptA[:, 0:128], rqTq[:, f * 128:(f + 1) * 128],
                            ident[:])), waits=[e_rd, prev_trB[0], e_gc])
            e_e = P("V", lambda v, att, f=f, c=c: att(
                v.tensor_copy(out=qrhs[:, f * D + c * 128:f * D + (c + 1) * 128],
                              in_=ptA[:, 0:128])), waits=[e_t])
            prev_trB[0] = e_e
            e_qrhs_last[0] = e_e

    ev_rsrow = []
    prev = [prev_trF[0]]
    for c in range(8):
        e_t = P("T", lambda t, att, c=c: att(
            t.transpose(ptF[0:1, 0:128], rs_col[:, c:c + 1], identF[:])),
            waits=[prev[0], e_qrhs_last[0]])
        e_e = P("V", lambda v, att, c=c: att(
            v.tensor_copy(out=rs_row[0:1, c * 128:(c + 1) * 128],
                          in_=ptF[0:1, 0:128])), waits=[e_t])
        prev[0] = e_e
        ev_rsrow.append(e_e)

    def em_hilo(v, att):
        _chain("V", v, v.tensor_copy(out=rs_hi, in_=rs_row))
        att(v.tensor_sub(rs_lo, rs_row, rs_hi))
    e_hilo = P("V", em_hilo, waits=ev_rsrow)

    prev_rb = [None]
    for h0 in (0, 512):
        def em_mm(t, att, h0=h0):
            t.matmul(ptRB[:, 0:512], ones_row[:], rs_hi[0:1, h0:h0 + 512],
                     start=True, stop=False)
            att(t.matmul(ptRB[:, 0:512], ones_row[:], rs_lo[0:1, h0:h0 + 512],
                         start=False, stop=True))
        e_mm = P("T", em_mm, waits=[e_hilo, e_vc, prev_rb[0]])
        e_e = P("V", lambda v, att, h0=h0: att(
            v.tensor_copy(out=rs_bcast[:, h0:h0 + 512], in_=ptRB[:, 0:512])),
            waits=[e_mm])
        prev_rb[0] = e_e
    e_rsb_done = prev_rb[0]

    SETUP_STG_DONE = [slot_free[0], slot_free[1]] + list(rslot_free) + [ev_rsrow[-1], e_rsb_done]

    e_sinit = P("V", lambda v, att: att(v.memset(S_sb[:], 0.0)))

    ev_cast = {}
    ev_lastx = [None, None]
    prev_sq_mm = [None, None]
    prev_T_last = [None]
    prev_oev15 = [None]
    prev_o = [None, None]
    prev_supd = [None, None]
    prev_supd15 = [None]
    prev_mask = [None, None]
    prev_kt = [None, None]
    prev_ogev_last = [e_qrhs_last[0]]
    prev_ogmul = {}
    prev_ogt = [None, None]
    prev_rmsb7 = [None]
    pt_ev = {0: None, 1: None}
    prev_slab = [None]
    prev_bmm = {}
    prev_eq_reader = {}
    g_base = [plan.counts["G"]]

    WQK0 = ev_wq + ev_wk + [e_wgk2, e_bgk2, e_rsb_done, e_qrhs_last[0],
                            e_sinit, e_par]

    for tau in range(NT):
        T0 = tau * TILE
        buf = tau % 2
        xb = xbf[buf]

        for k in range(8):
            gidx = tau * 8 + k
            sl = gidx % 4
            if tau == 0:
                ev_cast[gidx] = pre_cast[k]
                continue
            e_l = P("Y", lambda y, att, k=k, sl=sl, T0=T0: att(
                y.dma_start(out=xf[:, sl * 512:(sl + 1) * 512],
                            in_=xT[k * 128:(k + 1) * 128, T0:T0 + TILE])),
                dma_n=1, dsem=f"sx{sl}",
                waits=[ev_cast.get(gidx - 4)])
            ev_cast[gidx] = P("S", lambda s, att, k=k, sl=sl, xb=xb: att(
                s.copy(out=xb[:, k * 512:(k + 1) * 512],
                       in_=xf[:, sl * 512:(sl + 1) * 512])),
                waits=[e_l, ev_lastx[buf]])
        e_casts = ev_cast[tau * 8 + 7]

        def em_sum(t, att, xb=xb):
            i = None
            for k in range(8):
                i = t.matmul(P2[0:1, 0:512], ones_col[:],
                             xb[:, k * 512:(k + 1) * 512],
                             start=(k == 0), stop=(k == 7))
            att(i)
        e_msum = P("T", em_sum,
                   waits=[e_casts, e_vc, prev_supd15[0]] +
                         (WQK0 if tau == 0 else [prev_T_last[0]]))

        e_mssq = None
        for k in range(8):
            e_square = P("S", lambda s, att, k=k, xb=xb: att(
                s.activation(out=xsq[:, (k % 2) * 512:(k % 2 + 1) * 512],
                             in_=xb[:, k * 512:(k + 1) * 512], func=AF.Square)),
                waits=[ev_cast[tau * 8 + k], prev_sq_mm[k % 2]])
            e_mssq = P("T", lambda t, att, k=k: att(
                t.matmul(P3[0:1, 0:512], ones_col[:],
                         xsq[:, (k % 2) * 512:(k % 2 + 1) * 512],
                         start=(k == 0), stop=(k == 7))),
                waits=[e_square] + ([e_msum] if k == 0 else []))
            prev_sq_mm[k % 2] = e_mssq

        def em_rows1(s, att):
            s.activation(out=m_rowf, in_=P2[0:1, 0:512], func=AF.Copy,
                         scale=1.0 / D)
            s.copy(out=m_rowb, in_=m_rowf)
            s.activation(out=e2_row, in_=P3[0:1, 0:512], func=AF.Copy,
                         scale=1.0 / D)
            att(s.activation(out=m_rowf, in_=m_rowf, func=AF.Square))
        e_rows1 = P("S", em_rows1, waits=[e_msum, e_mssq])

        e_var = P("V", lambda v, att: att(
            v.tensor_sub(e2_row, e2_row, m_rowf)), waits=[e_rows1])

        def em_rows2(s, att):
            s.activation(out=e2_row, in_=e2_row, func=AF.Ln, bias=EPS)
            s.activation(out=rstd_row, in_=e2_row, func=AF.Exp, scale=-0.5)
            s.activation(out=rr_rowb, in_=e2_row, func=AF.Exp, scale=0.5)
            att(s.copy(out=rhi_row, in_=rstd_row))
        e_rows2 = P("S", em_rows2, waits=[e_var])

        e_rlo = P("V", lambda v, att: att(
            v.tensor_sub(rlo_row, rstd_row, rhi_row[:])), waits=[e_rows2])

        def em_cols(t, att):
            i = None
            for tc in range(NTC):
                t.matmul(P2[:, 256 + tc:257 + tc],
                         rhi_row_t[0:1, tc * 128:(tc + 1) * 128],
                         ones_col[0:1, 0:1], start=True, stop=False)
                i = t.matmul(P2[:, 256 + tc:257 + tc],
                             rlo_row_t[0:1, tc * 128:(tc + 1) * 128],
                             ones_col[0:1, 0:1], start=False, stop=True)
            att(i)
        e_cols = P("T", em_cols, waits=[e_rlo, e_rows1])

        def em_colsS(s, att):
            s.copy(out=rstd_c[:], in_=P2[:, 256:256 + NTC])
            s.activation(out=nrstd_c[:], in_=rstd_c[:], func=AF.Copy, scale=-1.0)
            att(s.activation(out=rstd2_c[:], in_=rstd_c[:], func=AF.Square))
        e_colsS = P("S", em_colsS, waits=[e_cols])

        def em_mb(t, att):
            att(t.matmul(P3[:, 0:512], ones_row[:], m_rowb,
                         start=True, stop=True))
        e_mb = P("T", em_mb, waits=[e_rows1])
        e_mbev = P("V", lambda v, att: att(
            v.tensor_copy(out=mb_sb[:], in_=P3[:, 0:512])), waits=[e_mb])

        def em_xsub(v, att, xb=xb):
            i = None
            for k in range(8):
                i = v.tensor_sub(xb[:, k * 512:(k + 1) * 512],
                                 xb[:, k * 512:(k + 1) * 512], mb_sb[:])
            att(i)
        e_xsub = P("V", em_xsub,
                   waits=[e_mbev, prev_sq_mm[0], prev_sq_mm[1]])

        evac_evs = []
        gi = [0]

        def add_fm_group(wsb, dstsb, m, extra_evac_waits):
            bank = [P0, P1][gi[0] % 2]
            bank_wait = evac_evs[gi[0] - 2] if gi[0] >= 2 else prev_mask[gi[0]]

            def em_mm(t, att, wsb=wsb, m=m, bank=bank, xb=xb):
                i = None
                for k in range(8):
                    i = t.matmul(bank[:, 0:512],
                                 wsb[:, k * 512 + m * 128:k * 512 + (m + 1) * 128],
                                 xb[:, k * 512:(k + 1) * 512],
                                 start=(k == 0), stop=(k == 7))
                att(i)
            e_mm = P("T", em_mm, waits=[e_xsub, bank_wait])

            def em_ev(v, att, dstsb=dstsb, m=m, bank=bank):
                att(v.tensor_copy(out=dstsb[:, m * 512:(m + 1) * 512],
                                  in_=bank[:, 0:512]))
            e_ev = P("V", em_ev, waits=[e_mm] + extra_evac_waits)
            evac_evs.append(e_ev)
            gi[0] += 1
            return e_ev

        def add_tm_group(wsb, dstsb, tc, half, kind, extra_evac_waits):
            bank = [P0, P1][gi[0] % 2]
            bank_wait = evac_evs[gi[0] - 2] if gi[0] >= 2 else prev_mask[gi[0]]

            def em_mm(t, att, wsb=wsb, tc=tc, half=half, bank=bank,
                      xb=xb):
                i = None
                for k in range(8):
                    i = t.matmul(bank[:, 0:512],
                                 xb[:, k * 512 + tc * 128:k * 512 + (tc + 1) * 128],
                                 wsb[:, k * 1024 + half * 512:k * 1024 + (half + 1) * 512],
                                 start=(k == 0), stop=(k == 7))
                att(i)
            e_mm = P("T", em_mm, waits=[e_xsub, bank_wait])

            if kind == "v":
                def em_ev(v, att, dstsb=dstsb, tc=tc, half=half, bank=bank):
                    att(v.tensor_scalar(
                        out=dstsb[:, tc * 1024 + half * 512:
                                  tc * 1024 + (half + 1) * 512],
                        in0=bank[:, 0:512], scalar1=rstd2_c[:, tc:tc + 1],
                        scalar2=None, op0=ALU.mult))
                e_ev = P("V", em_ev, waits=[e_mm, e_colsS] + extra_evac_waits)
            else:
                def em_ev(s, att, dstsb=dstsb, tc=tc, half=half, bank=bank):
                    att(s.activation(
                        out=dstsb[:, tc * 1024 + half * 512:
                                  tc * 1024 + (half + 1) * 512],
                        in_=bank[:, 0:512], func=AF.Silu,
                        scale=rstd_c[:, tc:tc + 1]))
                e_ev = P("S", em_ev, waits=[e_mm, e_colsS] + extra_evac_waits)
            evac_evs.append(e_ev)
            gi[0] += 1
            return e_ev

        qk_extra = [prev_o[0], prev_o[1]]
        qk_extra = [prev_eq_reader.get(3)]
        for m in range(4):
            add_fm_group(wq, qb, m, qk_extra)
        for m in range(4):
            add_fm_group(wk, kb, m, qk_extra)

        def em_t17(t, att, xb=xb):
            i = None
            for k in range(8):
                i = t.matmul(P5[0:16, 0:512], wgk1[:, k * 16:(k + 1) * 16],
                             xb[:, k * 512:(k + 1) * 512],
                             start=(k == 0), stop=(k == 7))
            att(i)
        e_t17 = P("T", em_t17, waits=[e_xsub, prev_oev15[0]])

        e_t17ev = P("V", lambda v, att: att(
            v.tensor_copy(out=t17[:], in_=P5[0:16, 0:512])),
            waits=[e_t17, e_rows2])

        ev_eq = {}
        ev_bmm = {}
        v_list = [(tc, h) for tc in range(NTC) for h in range(2)]
        g_list = list(v_list)
        vi = 0
        gk_step = 0
        order = []
        while vi < len(v_list) or gk_step < 2 * NTC:
            for _ in range(2):
                if vi < len(v_list):
                    order.append(("v", v_list[vi]))
                    vi += 1
            if gk_step < 2 * NTC:
                order.append(("gk", gk_step))
                gk_step += 1

        for kind, info in order:
            if kind == "v":
                tc, h = info
                add_tm_group(wv, vb, tc, h, "v", [prev_T_last[0]])
            else:
                step = info
                tc = step // 2
                if step % 2 == 0:
                    def em_gkmm(t, att, tc=tc):
                        t.matmul(P4[:, 0:512],
                                 t17[0:16, tc * 128:(tc + 1) * 128],
                                 wgk2h[:], start=True, stop=False)
                        att(t.matmul(P4[:, 0:512],
                                     rr_rowb_t[0:1, tc * 128:(tc + 1) * 128],
                                     bgk2b[:], start=False, stop=True))
                    e_gkmm = P("T", em_gkmm,
                               waits=[e_t17ev,
                                      ev_eq.get(tc - 1) or prev_eq_reader.get("eq3")])

                    def em_usp(s, att, tc=tc):
                        s.activation(out=u_f[:], in_=P4[:, 0:512], func=AF.Exp,
                                     scale=nrstd_c[:, tc:tc + 1])
                        att(s.activation(out=spb[tc][:], in_=u_f[:],
                                         func=AF.Ln, bias=1.0))
                    globals()["_e_usp%d" % tc] = P(
                        "S", em_usp,
                        waits=[e_gkmm, e_colsS, prev_bmm.get(tc)])
                else:
                    e_usp = globals()["_e_usp%d" % tc]

                    def em_bmm(t, att, tc=tc):
                        i = None
                        for cst in range(4):
                            i = t.matmul(P4[:, cst * 128:(cst + 1) * 128],
                                         spb[tc][:, cst * 128:(cst + 1) * 128],
                                         lneg[:], start=True, stop=True)
                        att(i)
                    e_bmm = P("T", em_bmm, waits=[e_usp, e_gc])
                    ev_bmm[tc] = e_bmm

                    def em_eq(s, att, tc=tc):
                        s.activation(out=eqb[tc][:], in_=P4[:, 0:512],
                                     func=AF.Exp, bias=-LN16)
                        s.activation(out=ekb[tc][:], in_=P4[:, 0:512],
                                     func=AF.Exp, scale=-1.0)
                        for h in range(2):
                            idx = tc * 2 + h
                            off = 2 * h * 128 + 127
                            s.activation(
                                out=eblA[:, idx * 2:idx * 2 + 2],
                                in_=P4[:, off:off + 129:128],
                                func=AF.Exp)
                        _chain("S", s, s.copy(out=blr[:, 0:4],
                                              in_=P4[:, 127:512:128]))
                        i = None
                        for sl4 in range(4):
                            i = s.activation(
                                out=ekb2[tc][:, sl4 * 128:(sl4 + 1) * 128],
                                in_=P4[:, sl4 * 128:(sl4 + 1) * 128],
                                func=AF.Exp, scale=-1.0,
                                bias=blr[:, sl4:sl4 + 1])
                        att(i)
                    ev_eq[tc] = P("S", em_eq,
                                  waits=[e_bmm, prev_eq_reader.get(tc),
                                         prev_supd15[0]])

        for tc, h in g_list:
            add_tm_group(wg, sgb, tc, h, "g",
                         [prev_ogmul.get(tc * 2 + h),
                          prev_T_last[0]])

        e_vg_last = evac_evs[-1]
        ev_lastx[buf] = evac_evs[-1]
        ev_lastx[buf] = None

        last_qg = {}
        last_ki = {}
        last_mask = {}
        last_ktev = {}
        last_kt = {}
        last_ki2 = {}
        last_supd = {}
        last_oev = {}
        last_o = {}
        last_rmsb = {}
        last_ogmul = {}
        last_ogt = {}
        ogev_tile = []

        def emit_ogt(j):
            tcj, hj = divmod(j, 2)
            ob = [ogtm, ogtm2][j % 2]
            e_og_j = last_ogmul[j]
            last = None
            for dvs in range(4):
                bo = dvs % 2
                bank = [ptA, ptB][bo]

                def em_ogt(t, att, dvs=dvs, bank=bank, ob=ob):
                    att(t.transpose(bank[:, 128:256],
                                    ob[:, dvs * 128:(dvs + 1) * 128],
                                    ident[:]))
                e_ogt = P("T", em_ogt, waits=[e_og_j, pt_ev[bo]])
                dchunk = hj * 4 + dvs

                def em_ogev(v, att, dchunk=dchunk, tcj=tcj, bank=bank):
                    att(v.tensor_copy(
                        out=ogfm[:, dchunk * 512 + tcj * 128:
                                 dchunk * 512 + (tcj + 1) * 128],
                        in_=bank[:, 128:256]))
                e_ogev = P("V", em_ogev,
                           waits=[e_ogt, prev_slab[0]
                                  if (j == 0 and dvs == 0) else None])
                pt_ev[bo] = e_ogev
                ogev_tile.append(e_ogev)
                last = e_ogt
            last_ogt[j] = last
        for i in range(8):
            tc, h = divmod(i, 2)
            pp = i % 2
            blk = i
            eq_ev = ev_eq[tc]

            def em_qg(v, att, tc=tc, h=h, pp=pp):
                ii = None
                for half in range(2):
                    base = (2 * h + half) * 512 + tc * 128
                    ii = v.tensor_mul(
                        qg[pp][:, half * 128:(half + 1) * 128],
                        qb[:, base:base + 128],
                        eqb[tc][:, (2 * h + half) * 128:
                                (2 * h + half + 1) * 128])
                att(ii)
            e_qg = P("V", em_qg,
                     waits=[eq_ev, last_o.get(i - 2) or prev_o[i % 2]])

            def em_ki(v, att, tc=tc, h=h, pp=pp):
                ii = None
                for half in range(2):
                    base = (2 * h + half) * 512 + tc * 128
                    ii = v.tensor_mul(
                        ki[pp][:, half * 128:(half + 1) * 128],
                        kb[:, base:base + 128],
                        ekb[tc][:, (2 * h + half) * 128:
                                (2 * h + half + 1) * 128])
                att(ii)
            e_ki = P("V", em_ki,
                     waits=[eq_ev, last_o.get(i - 2) or prev_o[i % 2]])

            def em_ki2(v, att, tc=tc, h=h, pp=pp):
                ii = None
                for half in range(2):
                    base = (2 * h + half) * 512 + tc * 128
                    ii = v.tensor_mul(
                        ki2[pp][:, half * 128:(half + 1) * 128],
                        kb[:, base:base + 128],
                        ekb2[tc][:, (2 * h + half) * 128:
                                 (2 * h + half + 1) * 128])
                att(ii)
            e_ki2 = P("V", em_ki2,
                      waits=[eq_ev, last_kt.get(i - 2) or prev_kt[i % 2]])
            last_qg[i] = e_qg
            last_ki[i] = e_ki
            last_ki2[i] = e_ki2

            sc_bank = [P0, P1][i % 2]

            def em_sc(t, att, pp=pp, sc_bank=sc_bank):
                t.matmul(sc_bank[:, 0:128],
                         ki[pp][:, 0:128], qg[pp][:, 0:128],
                         start=True, stop=False)
                att(t.matmul(sc_bank[:, 0:128],
                             ki[pp][:, 128:256], qg[pp][:, 128:256],
                             start=False, stop=True))
            e_sc = P("T", em_sc,
                     waits=[e_qg, e_ki, e_vg_last,
                            last_mask.get(i - 2) or prev_mask[i % 2]])

            def em_mask(v, att, pp=pp, sc_bank=sc_bank):
                att(v.tensor_mul(stb[pp][:], sc_bank[:, 0:128], mask128[:]))
            e_mask = P("V", em_mask, waits=[e_sc])
            last_mask[i] = e_mask

            kt_bank = [ptA, ptB][i % 2]

            def em_kt(t, att, pp=pp, kt_bank=kt_bank):
                t.transpose(kt_bank[:, 0:128], ki2[pp][:, 0:128], ident[:])
                att(t.transpose(kt_bank[:, 128:256], ki2[pp][:, 128:256],
                                ident[:]))
            e_kt = P("T", em_kt,
                     waits=[e_ki2, pt_ev[i % 2] or prev_slab[0] or
                            e_qrhs_last[0]])
            last_kt[i] = e_kt

            def em_ktev(v, att, pp=pp, kt_bank=kt_bank):
                att(v.tensor_copy(out=ktb[pp][:], in_=kt_bank[:, 0:256]))
            e_ktev = P("V", em_ktev, waits=[e_kt])
            last_ktev[i] = e_ktev
            pt_ev[i % 2] = e_ktev

            def em_ds(t, att, tc=tc, h=h, pp=pp):
                t.matmul(P2[:, 0:512], ktb[pp][:, 0:128],
                         vb[:, tc * 1024 + h * 512:tc * 1024 + (h + 1) * 512],
                         start=True, stop=True)
                att(t.matmul(P3[:, 0:512], ktb[pp][:, 128:256],
                             vb[:, tc * 1024 + h * 512:tc * 1024 + (h + 1) * 512],
                             start=True, stop=True))
            e_ds = P("T", em_ds,
                     waits=[e_ktev, e_colsS,
                            last_supd.get(i - 1) or prev_supd15[0]])

            def em_o(t, att, tc=tc, h=h, pp=pp):
                orow = P5[:, 0:512]
                t.matmul(orow, stb[pp][:],
                         vb[:, tc * 1024 + h * 512:tc * 1024 + (h + 1) * 512],
                         start=True, stop=False)
                t.matmul(orow, qg[pp][:, 0:128],
                         S_sb[:, (2 * h) * 512:(2 * h + 1) * 512],
                         start=False, stop=False)
                att(t.matmul(orow, qg[pp][:, 128:256],
                             S_sb[:, (2 * h + 1) * 512:(2 * h + 2) * 512],
                             start=False, stop=True))
            e_o = P("T", em_o,
                    waits=[e_mask,
                           last_supd.get(i - 2) or prev_supd[i % 2],
                           last_oev.get(i - 1) or prev_oev15[0],
                           e_t17ev if i == 0 else None])
            last_o[i] = e_o

            def em_oev(s, att, tc=tc, h=h):
                blk = tc * 2 + h
                s.activation(out=o_park[:, blk * 512:(blk + 1) * 512],
                             in_=P5[:, 0:512], func=AF.Copy,
                             scale=rstd_c[:, tc:tc + 1])
                att(s.activation(
                    out=osq[:],
                    in_=o_park[:, blk * 512:(blk + 1) * 512],
                    func=AF.Square,
                    accum_out=rmsA[:, blk:blk + 1]))
            e_oev = P("S", em_oev, waits=[e_o, prev_ogmul.get(i)])
            last_oev[i] = e_oev

            def em_supd(v, att, h=h, pp=pp, i=i):
                ii = None
                for half, bank in ((0, P2), (1, P3)):
                    sl = (2 * h + half)
                    ii = v.scalar_tensor_tensor(
                        out=S_sb[:, sl * 512:(sl + 1) * 512],
                        in0=S_sb[:, sl * 512:(sl + 1) * 512],
                        scalar=eblA[:, i * 2 + half:i * 2 + half + 1],
                        in1=bank[:, 0:512],
                        op0=ALU.mult, op1=ALU.add)
                att(ii)
            e_supd = P("V", em_supd, waits=[e_ds, e_o])
            last_supd[i] = e_supd

        for j in range(8):
            tcj, hj = divmod(j, 2)

            def em_rmsb(s, att, blk=j):
                s.activation(out=rmsB[:, blk:blk + 1],
                             in_=rmsA[:, blk:blk + 1],
                             func=AF.Ln, scale=1.0 / DV, bias=EPS)
                att(s.activation(out=rmsB[:, blk:blk + 1],
                                 in_=rmsB[:, blk:blk + 1],
                                 func=AF.Exp, scale=-0.5))
            e_rmsb = P("S", em_rmsb, waits=[last_oev[j]])
            last_rmsb[j] = e_rmsb

            ob = [ogtm, ogtm2][j % 2]

            def em_og(v, att, tcj=tcj, hj=hj, blk=j, ob=ob):
                att(v.scalar_tensor_tensor(
                    out=ob[:], in0=o_park[:, blk * 512:(blk + 1) * 512],
                    scalar=rmsB[:, blk:blk + 1],
                    in1=sgb[:, tcj * 1024 + hj * 512:
                            tcj * 1024 + (hj + 1) * 512],
                    op0=ALU.mult, op1=ALU.mult))
            e_og = P("V", em_og,
                     waits=[e_rmsb, evac_evs[len(evac_evs) - 8 + j],
                            last_ogt.get(j - 2) or prev_ogt[j % 2]])
            last_ogmul[j] = e_og
            emit_ogt(j)

        gbase = plan.counts.get("gw", 0)

        def em_slab(g, att, tau=tau, T0=T0, gbase=gbase):
            with g.register(f"p{tau}") as p_reg:
                g.reg_load(p_reg, par_sb[0:1, 0:1])
                for side in range(2):
                    ctx = g.If_eq(p_reg, 0) if side == 0 else g.Else()
                    with ctx:
                        base = side * D
                        for d in range(8):
                            ii = g.dma_start(
                                out=sdata.ap()[base + d * 128:base + (d + 1) * 128,
                                               T0:T0 + TILE],
                                in_=ogfm[:, d * 512:(d + 1) * 512])
                            if side == 0:
                                att(ii)
                            else:
                                ii.then_inc(SEMS["gw"], 16)
                        g.wait_ge(SEMS["gw"], gbase + 128)
                        ii = g.dma_start(
                            out=sflag.ap()[side:side + 1, tau:tau + 1],
                            in_=magic_sb[0:1, 0:1])
                        if side == 0:
                            att(ii)
                        else:
                            ii.then_inc(SEMS["gw"], 16)
        e_sl = P("G", em_slab, waits=ogev_tile + [e_par, prev_slab[0]],
                 dma_n=9, dsem="gw")

        prev_T_last[0] = last_o[7]
        prev_oev15[0] = last_oev[7]
        prev_o = [last_o[6], last_o[7]]
        prev_supd = [last_supd[6], last_supd[7]]
        prev_supd15[0] = last_supd[7]
        prev_mask = [last_mask[6], last_mask[7]]
        prev_kt = [last_kt[6], last_kt[7]]
        prev_ogmul = dict(last_ogmul)
        prev_ogt = [last_ogt[6], last_ogt[7]]
        prev_rmsb7[0] = last_rmsb[7]
        prev_slab[0] = e_sl
        prev_ogev_last[0] = ogev_tile[-1]
        for tc in range(NTC):
            prev_bmm[tc] = ev_bmm[tc]
            prev_eq_reader[tc] = last_ki2[tc * 2 + 1]
        prev_eq_reader["eq3"] = ev_eq[3]
        ev_lastx[buf] = last_o[7]

    ph1_T = prev_T_last[0]
    ph1_S_ev = prev_rmsb7[0]
    ph1_V_ev = prev_ogev_last[0]
    ph1_G_ev = prev_slab[0]

    A0 = P0
    A1 = P1

    wo_slot_free = [None, None]
    wo_buf_free = [ph1_T, None, None, ph1_G_ev]
    ev_wo = []
    for k in range(16):
        sl = k % 2
        e_l = P("Y", lambda y, att, k=k, sl=sl: att(
            y.dma_start(out=stg[:, sl * 1024:(sl + 1) * 1024],
                        in_=wo_d[k * 128:(k + 1) * 128, :])),
            dma_n=1, dsem=f"st{sl}",
            waits=[wo_slot_free[sl]])
        e_c = P("V", lambda v, att, k=k, sl=sl: att(
            v.tensor_scalar(out=wo_slice(k, 0, 1024),
                            in0=stg[:, sl * 1024:(sl + 1) * 1024],
                            scalar1=gnwc_sb[:, k % 4:k % 4 + 1], scalar2=None,
                            op0=ALU.mult)),
            waits=[e_l, e_gnw, wo_buf_free[k // 4]])
        wo_slot_free[sl] = e_c
        ev_wo.append(e_c)

    last_attn = {}
    last_x1 = {}
    last_t1 = {}
    last_qtrV = {}
    last_round = {}
    last_ssq2 = {}
    last_msq2 = {}
    last_rstd2 = {}
    last_sub = {}
    last_out = {}
    last_st = {}
    last_srd = [None]
    srd_ev = {}
    srdA_ev = {}
    fr_xl = {}

    def AAof(ot, tc):
        return [(P0, P1), (P2, P3), (P4, P5)][(ot * NTC + tc) % 3]

    def mk_srd(ot):
        def em_srdA(g, att, ot=ot):
            with g.register(f"q{ot}a") as p_reg, \
                 g.register(f"f{ot}a") as f_reg, \
                 g.register(f"n{ot}a") as n_reg:
                g.reg_load(p_reg, par_sb[0:1, 0:1])
                for side in range(2):
                    ctx = g.If_eq(p_reg, 0) if side == 0 else g.Else()
                    with ctx:
                        gt = side * NOT + ot
                        other = 1 - side
                        g.reg_mov(f_reg, 0)
                        g.reg_alu(n_reg, f_reg, FLAG, ALU.not_equal)
                        with g.While(n_reg):
                            g.dma_start(
                                out=flagrx[0:1, 0:1],
                                in_=sflag.ap()[other:other + 1, gt:gt + 1]
                            ).then_inc(SEMS["P"], 16)
                            g.nop()
                            g.reg_load(f_reg, flagrx[0:1, 0:1])
                            g.reg_alu(n_reg, f_reg, FLAG, ALU.not_equal)
                        T0 = gt * TILE
                        for gch in range(8):
                            ii = g.dma_start(
                                out=xbf[0][:, gch * 512:(gch + 1) * 512],
                                in_=sdata.ap()[gch * 128:(gch + 1) * 128,
                                               T0:T0 + TILE])
                            if side == 0:
                                att(ii)
                            else:
                                ii.then_inc(SEMS["gr"], 16)
        e_srdA = P("G", em_srdA,
                   waits=[ph1_G_ev, ph1_T, ph1_S_ev, ph1_V_ev, last_srd[0],
                          last_attn.get((ot - 1, 3))], dma_n=8, dsem="gr")

        def em_srdB(g, att, ot=ot):
            with g.register(f"q{ot}b") as p_reg:
                g.reg_load(p_reg, par_sb[0:1, 0:1])
                for side in range(2):
                    ctx = g.If_eq(p_reg, 0) if side == 0 else g.Else()
                    with ctx:
                        gt = side * NOT + ot
                        T0 = gt * TILE
                        for gch in range(8, 16):
                            ii = g.dma_start(
                                out=xbf[1][:, (gch % 8) * 512:
                                           (gch % 8 + 1) * 512],
                                in_=sdata.ap()[gch * 128:(gch + 1) * 128,
                                               T0:T0 + TILE])
                            if side == 0:
                                att(ii)
                            else:
                                ii.then_inc(SEMS["gr"], 16)
        e_srdB = P("G", em_srdB, waits=[], dma_n=8, dsem="gr")
        last_srd[0] = e_srdB
        srd_ev[ot] = e_srdB
        srdA_ev[ot] = e_srdA

    def mk_front(ot, tc):
        pb = tc % 2
        Aa, Ab = AAof(ot, tc)
        e_xl = P("Y", lambda y, att, ot=ot, tc=tc, pb=pb: att(
            y.dma_start(out=xo[:, pb * 1024:(pb + 1) * 1024],
                        in_=xown[ot * TILE + tc * 128:
                                 ot * TILE + (tc + 1) * 128, :])),
            dma_n=1, dsem=f"xo{pb}",
            waits=[ph1_T, ph1_S_ev, ph1_V_ev, ph1_G_ev, ev_wo[-1],
                   last_x1.get((ot, tc - 2)) or last_x1.get((ot - 1, tc + 2))])
        fr_xl[(ot, tc)] = e_xl

        bank_gate = (last_t1.get(divmod(ot * NTC + tc - 3, NTC))
                     if ot * NTC + tc >= 3 else None)
        if tc == 0:
            def em_attnA(t, att, tc=tc, Aa=Aa, Ab=Ab):
                i = None
                for nh, bank in ((0, Aa), (1, Ab)):
                    for gch in range(8):
                        i = t.matmul(
                            bank[:, 0:512],
                            xbf[0][:, gch * 512 + tc * 128:
                                    gch * 512 + (tc + 1) * 128],
                            wo_slice(gch, nh * 512, (nh + 1) * 512),
                            start=(gch == 0), stop=False)
                att(i)
            e_attnA = P("T", em_attnA,
                        waits=[srdA_ev[ot], ph1_V_ev, ph1_S_ev, ev_wo[-1],
                               ev_wo[-2], bank_gate])

            def em_attnB(t, att, tc=tc, Aa=Aa, Ab=Ab):
                i = None
                for nh, bank in ((0, Aa), (1, Ab)):
                    for gch in range(8, 16):
                        i = t.matmul(
                            bank[:, 0:512],
                            xbf[1][:, (gch % 8) * 512 + tc * 128:
                                    (gch % 8) * 512 + (tc + 1) * 128],
                            wo_slice(gch, nh * 512, (nh + 1) * 512),
                            start=False, stop=(gch == 15))
                att(i)
            e_attn = P("T", em_attnB, waits=[srd_ev[ot]])
        else:
            def em_attn(t, att, tc=tc, Aa=Aa, Ab=Ab):
                i = None
                for nh, bank in ((0, Aa), (1, Ab)):
                    for gch in range(16):
                        i = t.matmul(
                            bank[:, 0:512],
                            xbf[gch // 8][:, (gch % 8) * 512 + tc * 128:
                                          (gch % 8) * 512 + (tc + 1) * 128],
                            wo_slice(gch, nh * 512, (nh + 1) * 512),
                            start=(gch == 0), stop=(gch == 15))
                att(i)
            e_attn = P("T", em_attn,
                       waits=[srd_ev[ot], ph1_V_ev, ph1_S_ev, ev_wo[-1],
                              ev_wo[-2], bank_gate])
        last_attn[(ot, tc)] = e_attn

    def mk_back(ot, tc):
        pb = tc % 2
        Aa, Ab = AAof(ot, tc)
        e_attn = last_attn[(ot, tc)]
        e_xl = fr_xl[(ot, tc)]
        if True:
            def em_x1(v, att, pb=pb, Aa=Aa, Ab=Ab):
                v.tensor_add(x1b[:, pb * 1024:pb * 1024 + 512], Aa[:, 0:512],
                             xo[:, pb * 1024:pb * 1024 + 512])
                att(v.tensor_add(x1b[:, pb * 1024 + 512:(pb + 1) * 1024],
                                 Ab[:, 0:512],
                                 xo[:, pb * 1024 + 512:(pb + 1) * 1024]))
            e_x1 = P("V", em_x1,
                     waits=[e_attn, e_xl,
                            last_ssq2.get((ot, tc - 2)) or
                            last_ssq2.get((ot - 1, tc + 2))])
            last_x1[(ot, tc)] = e_x1

            e_red = P("V", lambda v, att, pb=pb: att(
                v.reduce_sum(s2a[:, pb:pb + 1],
                             x1b[:, pb * 1024:(pb + 1) * 1024], axis=AX.X)),
                waits=[e_x1])

            def em_ssq2(s, att, pb=pb):
                att(s.activation(out=qfm[:, pb * 1024:(pb + 1) * 1024],
                                 in_=x1b[:, pb * 1024:(pb + 1) * 1024],
                                 func=AF.Square, accum_out=ssqa[:, pb:pb + 1]))
            e_ssq2 = P("S", em_ssq2, waits=[e_x1])
            last_ssq2[(ot, tc)] = e_ssq2

            def em_m2(v, att, pb=pb):
                v.tensor_scalar(out=m2a[:, pb:pb + 1], in0=s2a[:, pb:pb + 1],
                                scalar1=1.0 / D, scalar2=None, op0=ALU.mult)
                att(v.tensor_scalar(out=var2a[:, pb:pb + 1],
                                    in0=ssqa[:, pb:pb + 1],
                                    scalar1=1.0 / D, scalar2=None, op0=ALU.mult))
            e_m2 = P("V", em_m2,
                     waits=[e_red, e_ssq2,
                            last_msq2.get((ot, tc - 2)) or
                            last_msq2.get((ot - 1, tc + 2)),
                            last_rstd2.get((ot, tc - 2)) or
                            last_rstd2.get((ot - 1, tc + 2))])

            e_msq2 = P("S", lambda s, att, pb=pb: att(
                s.activation(out=msqa[:, pb:pb + 1], in_=m2a[:, pb:pb + 1],
                             func=AF.Square)), waits=[e_m2])
            last_msq2[(ot, tc)] = e_msq2

            e_var3 = P("V", lambda v, att, pb=pb: att(
                v.tensor_sub(var2a[:, pb:pb + 1], var2a[:, pb:pb + 1],
                             msqa[:, pb:pb + 1])), waits=[e_msq2])

            def em_rstd2(s, att, pb=pb):
                s.activation(out=var2a[:, pb:pb + 1], in_=var2a[:, pb:pb + 1],
                             func=AF.Ln, bias=EPS)
                att(s.activation(out=rstd2a[:, pb:pb + 1],
                                 in_=var2a[:, pb:pb + 1],
                                 func=AF.Exp, scale=-0.5))
            e_rstd2 = P("S", em_rstd2, waits=[e_var3])
            last_rstd2[(ot, tc)] = e_rstd2

            def em_sub(v, att, pb=pb):
                att(v.tensor_scalar(out=subb[:, pb * 1024:(pb + 1) * 1024],
                                    in0=x1b[:, pb * 1024:(pb + 1) * 1024],
                                    scalar1=m2a[:, pb:pb + 1], scalar2=None,
                                    op0=ALU.subtract))
            e_sub = P("V", em_sub,
                      waits=[e_m2, last_round.get((ot, tc - 2)) or
                             last_round.get((ot - 1, tc + 2)),
                             last_st.get((ot, tc - 2)) or
                             last_st.get((ot - 1, tc + 2))])
            last_sub[(ot, tc)] = e_sub

            def em_amax(v, att, pb=pb):
                _chain("V", v, v.tensor_reduce(
                    out=amaxa[:, pb:pb + 1],
                    in_=subb[:, pb * 1024:(pb + 1) * 1024],
                    axis=AX.X, op=ALU.max, apply_absolute_value=True))
                att(v.tensor_scalar_max(
                    out=amaxa[:, pb:pb + 1], in0=amaxa[:, pb:pb + 1],
                    scalar1=1e-30))
            e_amax = P("V", em_amax, waits=[e_sub])

            def em_invq(s, att, pb=pb):
                s.activation(out=invqa[:, pb:pb + 1], in_=amaxa[:, pb:pb + 1],
                             func=AF.Ln)
                att(s.activation(out=invqa[:, pb:pb + 1],
                                 in_=invqa[:, pb:pb + 1],
                                 func=AF.Exp, scale=-1.0, bias=LN127))
            e_invq = P("S", em_invq, waits=[e_amax])

            def em_ls(v, att, pb=pb):
                att(v.scalar_tensor_tensor(
                    out=lsa[:, pb:pb + 1], in0=amaxa[:, pb:pb + 1],
                    scalar=1.0 / 127.0, in1=rstd2a[:, pb:pb + 1],
                    op0=ALU.mult, op1=ALU.mult))
            e_ls = P("V", em_ls,
                     waits=[e_amax, e_rstd2,
                            last_t1.get((ot, tc - 2)) or
                            last_t1.get((ot - 1, tc + 2))])

            e_qm = P("V", lambda v, att, pb=pb: att(
                v.tensor_scalar(out=subb[:, pb * 1024:(pb + 1) * 1024],
                                in0=subb[:, pb * 1024:(pb + 1) * 1024],
                                scalar1=invqa[:, pb:pb + 1], scalar2=None,
                                op0=ALU.mult)), waits=[e_invq])

            def em_round(s, att, pb=pb):
                s.activation(out=subb[:, pb * 1024:(pb + 1) * 1024],
                             in_=subb[:, pb * 1024:(pb + 1) * 1024],
                             func=AF.Copy, bias=MAGIC)
                s.activation(out=qib[:], in_=subb[:, pb * 1024:(pb + 1) * 1024],
                             func=AF.Copy, bias=-MAGIC)
                att(s.copy(out=qbfb[:, pb * 1024:(pb + 1) * 1024], in_=qib[:]))
            e_round = P("S", em_round,
                        waits=[e_qm, last_qtrV.get((ot, tc - 2)) or
                               last_qtrV.get((ot - 1, tc + 2))])
            last_round[(ot, tc)] = e_round

            qtr_evs = []
            for f in range(8):
                bank = [ptA, ptB][f % 2]

                def em_qt(t, att, f=f, pb=pb, bank=bank):
                    att(t.transpose(
                        bank[:, 256:384],
                        qbfb[:, pb * 1024 + f * 128:pb * 1024 + (f + 1) * 128],
                        ident[:]))
                e_qt = P("T", em_qt,
                         waits=[e_round, e_attn,
                                qtr_evs[-2] if len(qtr_evs) >= 2 else
                                (last_qtrV.get((ot, tc - 1)) or
                                 last_qtrV.get((ot - 1, 3)) or ph1_G_ev)])

                def em_qe(v, att, f=f, pb=pb, bank=bank):
                    att(v.tensor_copy(
                        out=qfm[:, pb * 1024 + f * 128:pb * 1024 + (f + 1) * 128],
                        in_=bank[:, 256:384]))
                e_qe = P("V", em_qe, waits=[e_qt])
                qtr_evs.append(e_qe)
            last_qtrV[(ot, tc)] = qtr_evs[-1]

            def em_ffn(t, att, pb=pb, Aa=Aa, Ab=Ab):
                i = None
                for nh, bank in ((0, Aa), (1, Ab)):
                    for f in range(8):
                        i = t.matmul(
                            bank[:, 0:512],
                            qfm[:, pb * 1024 + f * 128:pb * 1024 + (f + 1) * 128],
                            qrhs[:, f * D + nh * 512:f * D + (nh + 1) * 512],
                            start=(f == 0), stop=(f == 7))
                att(i)
            e_ffn = P("T", em_ffn, waits=qtr_evs + [e_x1])

            def em_t1(s, att, pb=pb, Aa=Aa, Ab=Ab):
                s.activation(out=t1b[:, 0:512], in_=Aa[:, 0:512], func=AF.Copy,
                             scale=lsa[:, pb:pb + 1])
                att(s.activation(out=t1b[:, 512:1024], in_=Ab[:, 0:512],
                                 func=AF.Copy, scale=lsa[:, pb:pb + 1]))
            e_t1 = P("S", em_t1,
                     waits=[e_ffn, e_ls, last_out.get((ot, tc - 1)) or
                            last_out.get((ot - 1, 3))])
            last_t1[(ot, tc)] = e_t1

            def em_out(v, att, pb=pb):
                v.tensor_mul(t1b[:], t1b[:], rs_bcast[:])
                att(v.tensor_add(subb[:, pb * 1024:(pb + 1) * 1024], t1b[:],
                                 x1b[:, pb * 1024:(pb + 1) * 1024]))
            e_out = P("V", em_out,
                      waits=[e_t1, last_st.get((ot, tc - 2)) or
                             last_st.get((ot - 1, tc + 2))])
            last_out[(ot, tc)] = e_out

            e_st = P("Y", lambda y, att, ot=ot, tc=tc, pb=pb: att(
                y.dma_start(out=out_d[ot * TILE + tc * 128:
                                      ot * TILE + (tc + 1) * 128, :],
                            in_=subb[:, pb * 1024:(pb + 1) * 1024])),
                dma_n=1, dsem=f"out{pb}", waits=[e_out])
            last_st[(ot, tc)] = e_st

    seq2 = [(o_, t_) for o_ in range(NOT) for t_ in range(NTC)]
    for idx2, (o_, t_) in enumerate(seq2):
        if idx2 >= 2:
            mk_back(*seq2[idx2 - 2])
        if t_ == 0:
            mk_srd(o_)
        mk_front(o_, t_)
    mk_back(*seq2[-2])
    mk_back(*seq2[-1])

    if _DEBUG:
        dbg_sdata = nc.dram_tensor("dbg_sdata", [2 * D, L], BF16,
                                   kind="ExternalOutput").ap()
        dbg_x1 = nc.dram_tensor("dbg_x1", [128, 2048], F32,
                                kind="ExternalOutput").ap()
        dbg_cols = nc.dram_tensor("dbg_cols", [128, 18], F32,
                                  kind="ExternalOutput").ap()
        dbg_sub = nc.dram_tensor("dbg_sub", [128, 2048], F32,
                                 kind="ExternalOutput").ap()
        dbg_rsb = nc.dram_tensor("dbg_rsb", [128, 1024], F32,
                                 kind="ExternalOutput").ap()
        dbg_qbf = nc.dram_tensor("dbg_qbf", [128, 2048], BF16,
                                 kind="ExternalOutput").ap()
        dbg_t1 = nc.dram_tensor("dbg_t1", [128, 1024], F32,
                                 kind="ExternalOutput").ap()
        dbg_qfm = nc.dram_tensor("dbg_qfm", [128, 2048], BF16,
                                 kind="ExternalOutput").ap()
        dbg_ident = nc.dram_tensor("dbg_ident", [128, 128], BF16,
                                   kind="ExternalOutput").ap()
        last_all = [last_st[(NOT - 1, tc)] for tc in range(NTC)] + \
                   [last_out[(NOT - 1, 3)], last_t1[(NOT - 1, 3)],
                    last_qtrV[(NOT - 1, 3)], ph1_G_ev]

        def em_dbg(y, att, n=[0]):
            att(y.dma_start(out=dbg_sdata, in_=sdata.ap()))
            att(y.dma_start(out=dbg_x1, in_=x1b[:]))
            att(y.dma_start(out=dbg_sub, in_=subb[:]))
            for i, t_ in enumerate((s2a, ssqa, m2a, var2a, rstd2a, amaxa,
                                    msqa, lsa, invqa)):
                att(y.dma_start(out=dbg_cols[:, i * 2:(i + 1) * 2], in_=t_[:]))
            att(y.dma_start(out=dbg_rsb, in_=rs_bcast[:]))
            att(y.dma_start(out=dbg_qbf, in_=qbfb[:]))
            att(y.dma_start(out=dbg_t1, in_=t1b[:]))
            att(y.dma_start(out=dbg_qfm, in_=qfm[:]))
            att(y.dma_start(out=dbg_ident, in_=ident[:]))
        e_dbg = P("Y", em_dbg, dma_n=17, dsem="dbg", waits=last_all)

    sem_names = (list("TSVGYP") + ["XV", "XS", "mp", "mg", "gw", "gr", "dbg"] +
                 [f"st{i}" for i in range(2)] + [f"rq{i}" for i in range(4)] +
                 [f"sx{i}" for i in range(4)] + ["xo0", "xo1", "out0", "out1"])
    with contextlib.ExitStack() as bs:
        for nm in sem_names:
            SEMS[nm] = bs.enter_context(nc.semaphore(f"sem_{nm}"))
        block = bs.enter_context(nc.Block())

        @block.tensor
        def _(t):
            plan.emit_engine("T", t, SEMS)

        @block.scalar
        def _(s):
            plan.emit_engine("S", s, SEMS)

        @block.vector
        def _(v):
            plan.emit_engine("V", v, SEMS)

        @block.gpsimd
        def _(g):
            plan.emit_engine("G", g, SEMS)

        @block.sync
        def _(y):
            plan.emit_engine("Y", y, SEMS)
            for p in (0, 1):
                y.wait_ge(SEMS[f"out{p}"], plan.counts.get(f"out{p}", 0))
            if _DEBUG:
                y.wait_ge(SEMS["dbg"], plan.counts.get("dbg", 0))

    es.close()
    return nc


_CACHE = {}


def _get_program(L):
    if L not in _CACHE:
        _CACHE[L] = build_program(L)
    return _CACHE[L]


def kernel(**inputs):
    x = np.ascontiguousarray(np.asarray(inputs["L0_attn_hidden_states"],
                                        dtype=np.float32))
    B, L, _ = x.shape
    Wq = np.asarray(inputs["L0_attn_Wq"], np.float32)
    Wk = np.asarray(inputs["L0_attn_Wk"], np.float32)
    Wv = np.asarray(inputs["L0_attn_Wv"], np.float32)
    Wg = np.asarray(inputs["L0_attn_Wg"], np.float32)
    Wgk1 = np.asarray(inputs["L0_attn_Wgk1"], np.float32)
    Wgk2 = np.asarray(inputs["L0_attn_Wgk2"], np.float32)
    bgk2 = np.asarray(inputs["L0_attn_bgk2"], np.float32).reshape(1, D)
    gnw = np.asarray(inputs["L0_attn_g_norm_w"], np.float32)
    Wo = np.asarray(inputs["L0_attn_Wo"], np.float32)
    rhs = np.asarray(inputs["L0_ffn_rhs"], np.float32)

    gnwc = np.ascontiguousarray(gnw.reshape(4, 128).T)
    nc = _get_program(L)

    in_maps = []
    for c in range(8):
        b, hp = divmod(c, 2)
        half = slice(hp * (L // 2), (hp + 1) * (L // 2))
        in_maps.append({
            "xT": np.ascontiguousarray(x[b].T),
            "xown": np.ascontiguousarray(x[b, half]),
            "wq": np.ascontiguousarray(Wq[:, hp * 512:(hp + 1) * 512]),
            "wk": np.ascontiguousarray(Wk[:, hp * 512:(hp + 1) * 512]),
            "wv": np.ascontiguousarray(Wv[:, hp * 1024:(hp + 1) * 1024]),
            "wg": np.ascontiguousarray(Wg[:, hp * 1024:(hp + 1) * 1024]),
            "wgk1": np.ascontiguousarray(Wgk1),
            "wgk2": np.ascontiguousarray(Wgk2[:, hp * 512:(hp + 1) * 512]),
            "bgk2": np.ascontiguousarray(bgk2[:, hp * 512:(hp + 1) * 512]),
            "wo": np.ascontiguousarray(Wo),
            "gnwc": gnwc,
            "rhs": np.ascontiguousarray(rhs),
            "parity": np.array([[hp]], dtype=np.int32),
        })
    res = run_bass_kernel_spmd(nc, in_maps, core_ids=list(range(8)))
    out = np.empty((B, L, D), np.float32)
    for c in range(8):
        b, hp = divmod(c, 2)
        out[b, hp * (L // 2):(hp + 1) * (L // 2)] = res.results[c]["out"]
    if _DEBUG:
        global _DBG_RES
        _DBG_RES = res.results
    return out
